# version 15
# baseline (speedup 1.0000x reference)
import contextlib

import numpy as np
import ml_dtypes

import concourse.bass as bass
import concourse.mybir as mybir
from concourse.bass_utils import run_bass_kernel_spmd

F32 = mybir.dt.float32
BF16 = mybir.dt.bfloat16
AF = mybir.ActivationFunctionType
AL = mybir.AluOpType
AX = mybir.AxisListType
bfnp = ml_dtypes.bfloat16

D = 1024
KB = 8
TT = 512
CH = 256
NHR = 4
DKR = 256
DVR = 512
NHI = 16
DHI = 64
NEG = -1.0e30
EPS = 1e-5

ENGS = ("t", "v", "s", "g", "y")
DMAQ = {"y": "yD", "s": "sD", "g": "gD"}
NDMASEM = 4
DMA_SEMS = {f"{q}{i}" for q in DMAQ.values() for i in range(NDMASEM)}


class BC:

    def __init__(self, nc):
        self.nc = nc
        self.es = contextlib.ExitStack()
        self.sems = {}
        for name in list(ENGS) + sorted(DMA_SEMS):
            self.sems[name] = self.es.enter_context(nc.semaphore("sem_" + name))
        self.cnt = {name: 0 for name in self.sems}
        self.qrot = {q: 0 for q in DMAQ.values()}
        self.queues = {e: [] for e in ENGS}
        self.hi = {}
        self.w = {}
        self.r = {}

    @staticmethod
    def _nk(x):
        if isinstance(x, bass.AP):
            return (x.tensor.name, None)
        if hasattr(x, "name"):
            return (x.name, None)
        if isinstance(x, tuple):
            return (x[0] if not hasattr(x[0], "name") else x[0].name,
                    x[1:] if len(x) > 2 else x[1])
        return (x, None)

    def _wdeps(self, name, slot):
        d = self.w.get(name)
        if not d:
            return []
        if slot is None:
            return list(d.values())
        out = []
        if None in d:
            out.append(d[None])
        if slot in d:
            out.append(d[slot])
        return out

    def _rdeps(self, name, slot):
        d = self.r.get(name)
        if not d:
            return []
        if slot is None:
            return [p for lst in d.values() for p in lst]
        return list(d.get(None, ())) + list(d.get(slot, ()))

    def op(self, eng, emit, reads=(), writes=(), dma=False, inc=True):
        need = {}

        def add(dep):
            if dep is None:
                return
            sem, v = dep
            if v > need.get(sem, 0):
                need[sem] = v
        rk = [self._nk(x) for x in reads]
        wk = [self._nk(x) for x in writes]
        for name, slot in rk:
            for dep in self._wdeps(name, slot):
                add(dep)
        for name, slot in wk:
            for dep in self._wdeps(name, slot):
                add(dep)
            for dep in self._rdeps(name, slot):
                add(dep)
        waits = []
        for sem, v in need.items():
            if self.hi.get((eng, sem), 0) < v:
                waits.append((sem, v))
                self.hi[(eng, sem)] = v
        if dma:
            q = DMAQ[eng]
            sem_name = f"{q}{self.qrot[q] % NDMASEM}"
            self.qrot[q] += 1
            prev = self.cnt[sem_name]
            if prev and self.hi.get((eng, sem_name), 0) < prev:
                waits.append((sem_name, prev))
                self.hi[(eng, sem_name)] = prev
        else:
            sem_name = eng
        prod = None
        if inc:
            self.cnt[sem_name] += 1
            prod = (sem_name, self.cnt[sem_name])
        self.queues[eng].append((waits, emit, sem_name if inc else None))
        for name, slot in wk:
            d = self.w.setdefault(name, {})
            if slot is None:
                d.clear()
            d[slot] = prod
            rr = self.r.get(name)
            if rr:
                if slot is None:
                    rr.clear()
                else:
                    rr.pop(slot, None)
        if prod:
            for name, slot in rk:
                self.r.setdefault(name, {}).setdefault(slot, []).append(prod)
        return prod

    def barrier(self):
        snap = dict(self.cnt)
        for eng in ENGS:
            waits = []
            for sem, v in snap.items():
                if v and self.hi.get((eng, sem), 0) < v:
                    waits.append((sem, v))
                    self.hi[(eng, sem)] = v
            if waits:
                self.queues[eng].append((waits, None, None))
        self.w.clear()
        self.r.clear()

    def flush(self):
        sems = self.sems

        def run(e, q):
            for waits, emit, sem_name in q:
                for s, v in waits:
                    step = 16 if s in DMA_SEMS else 1
                    e.wait_ge(sems[s], v * step)
                if emit is not None:
                    inst = emit(e)
                    if sem_name is not None:
                        inst.then_inc(sems[sem_name], 16 if sem_name in DMA_SEMS else 1)
        qs = self.queues
        with self.nc.Block() as block:
            block.tensor(lambda e: run(e, qs["t"]))
            block.vector(lambda e: run(e, qs["v"]))
            block.scalar(lambda e: run(e, qs["s"]))
            block.gpsimd(lambda e: run(e, qs["g"]))
            block.sync(lambda e: run(e, qs["y"]))
        self.queues = {e: [] for e in ENGS}

    @contextlib.contextmanager
    def section(self):
        with contextlib.ExitStack() as st:
            yield st
            self.barrier()



def host_consts(T):
    c = {}
    c["ident"] = np.eye(128, dtype=np.float32).astype(bfnp)
    c["ident32"] = np.eye(128, dtype=np.float32)
    t = np.arange(T, dtype=np.float64)
    invf = 1.0 / (10000.0 ** (np.arange(0, DKR, 2) / DKR))
    fr = invf[:, None] * t[None, :]
    c["rcos"] = np.cos(fr).astype(np.float32)
    c["rsin"] = np.sin(fr).astype(np.float32)
    gam = 1.0 - 2.0 ** (-5.0 - np.arange(NHR))
    i = np.arange(CH)
    diff = i[None, :] - i[:, None]
    mT = np.where(diff >= 0, gam[:, None, None] ** diff, 0.0) / 16.0
    c["maskT"] = np.ascontiguousarray(
        mT.reshape(NHR, 2, 128, CH).transpose(2, 0, 1, 3)).astype(bfnp)
    cd = (gam[:, None] ** (i + 1.0)) / 16.0
    c["cdq"] = np.ascontiguousarray(np.broadcast_to(cd[None], (128, NHR, CH))).astype(bfnp)
    kd = gam[:, None] ** (CH - 1.0 - i)
    c["kdec"] = np.ascontiguousarray(kd.reshape(NHR, 2, 128).transpose(2, 0, 1)).astype(np.float32)
    half = np.arange(0, DHI, 2, dtype=np.float64)
    invf2 = 1.0 / (10000.0 ** (half / DHI))
    fr2 = t[:, None] * invf2[None, :]
    base = (half + 0.4 * DHI) / (1.4 * DHI)
    power = (t - T // 2) / (256 // 2)
    scale = base[None, :] ** power[:, None]
    cos = np.cos(fr2)
    sin = np.sin(fr2)
    j = np.arange(DHI)
    sgn = np.where(j < 32, -1.0, 1.0)
    cosd = cos[:, j % 32]
    sind = sin[:, j % 32]
    scd = scale[:, j % 32]
    qa = (cosd * scd).T
    qb = sgn[:, None] * sind.T * scd.T
    ka = (cosd / scd).T
    kb_ = sgn[:, None] * sind.T / scd.T
    for name, tab in (("iqa", qa), ("iqb", qb), ("ika", ka), ("ikb", kb_)):
        c[name] = np.ascontiguousarray(np.vstack([tab, tab])).astype(np.float32)
    r = np.arange(128)[:, None]
    i1 = np.arange(128)[None, :]
    i2 = np.arange(256)[None, :]
    kb0 = np.where(i1 <= r, 0.0, NEG)
    kb3 = np.where(i1 >= r, 0.0, NEG)
    kb1 = np.where(i2 <= 128 + r, 0.0, NEG)
    kb2 = np.where(i2 >= r, 0.0, NEG)
    neg1 = np.full((128, 128), NEG)
    neg2 = np.full((128, 256), NEG)
    me = np.concatenate([kb0, kb3, kb0, kb3], axis=1)
    me0 = np.concatenate([neg1, kb3, neg1, kb3], axis=1)
    mm_ = np.concatenate([kb1, kb2], axis=1)
    mm0 = np.concatenate([neg2, kb2], axis=1)
    c["imask"] = np.ascontiguousarray(np.stack([me, me0, mm_, mm0], axis=1)).astype(bfnp)

    z = np.zeros((128, 33), dtype=np.float32)
    z[np.arange(128), 16 + np.arange(128) // 64] = 1.0
    c["zl2"] = z.astype(bfnp)
    bcl = np.zeros((16, KB * 128), dtype=np.float32)
    for b in range(KB):
        for m in range(128):
            bcl[2 * b + m // 64, b * 128 + m] = 1.0
    c["bclhs"] = bcl.astype(bfnp)
    z7 = np.zeros((128, 7), dtype=np.float32)
    z7[:, 3] = 1.0
    c["z7"] = z7.astype(bfnp)
    z7t4 = np.zeros((4, 4 * 128), dtype=np.float32)
    for h in range(4):
        z7t4[h, h * 128:(h + 1) * 128] = 1.0
    c["z7t"] = z7t4
    c["fconsts"] = np.broadcast_to(np.array([EPS, 1e-24, np.log(8.0)], dtype=np.float32), (128, 3)).copy()
    c["onescol"] = np.ones((128, 1), dtype=np.float32).astype(bfnp)
    c["onesrow32"] = np.ones((1, 128), dtype=np.float32)
    return c


def host_layout(ins, n_cores, T, nlayer):
    x = np.asarray(ins["L0_attn_x"], dtype=np.float32)
    B = x.shape[0]
    bpc = B // n_cores
    L = 6
    sh = {}
    sh["rWqT"] = np.ascontiguousarray(np.asarray(ins["L0_attn_Wq"], dtype=np.float32).T)
    sh["rWkT"] = np.ascontiguousarray(np.asarray(ins["L0_attn_Wk"], dtype=np.float32).T)
    sh["rWvT"] = np.ascontiguousarray(np.asarray(ins["L0_attn_Wv"], dtype=np.float32).T)
    sh["rWgT"] = np.ascontiguousarray(np.asarray(ins["L0_attn_Wg"], dtype=np.float32).T)
    sh["rWoT"] = np.ascontiguousarray(np.asarray(ins["L0_attn_Wo"], dtype=np.float32).T)
    sh["gnw4"] = np.ascontiguousarray(np.asarray(ins["L0_attn_gnorm_w"], dtype=np.float32).reshape(4, 128).T)
    sh["iWqkv"] = np.asarray(ins["L0_ffn_Wqkv"], dtype=np.float32)
    sh["iWo"] = np.asarray(ins["L0_ffn_Wo"], dtype=np.float32)
    W1 = np.asarray(ins["L0_ffn_W1"], dtype=np.float32)
    W1p = np.zeros((L, D, 2, 2816), dtype=np.float32)
    W1p[:, :, 0, :2730] = W1[:, :, :2730]
    W1p[:, :, 1, :2730] = W1[:, :, 2730:]
    sh["iW1p"] = np.ascontiguousarray(W1p.reshape(L, D, 5632))
    b1 = np.asarray(ins["L0_ffn_b1"], dtype=np.float32)
    b1p = np.zeros((L, 2, 2816), dtype=np.float32)
    b1p[:, 0, :2730] = b1[:, :2730]
    b1p[:, 1, :2730] = b1[:, 2730:]
    sh["ib1p"] = np.ascontiguousarray(b1p.reshape(L, 2, 22, 128).transpose(0, 3, 1, 2))
    W2 = np.asarray(ins["L0_ffn_W2"], dtype=np.float32)
    W2p = np.zeros((L, 2816, D), dtype=np.float32)
    W2p[:, :2730] = W2
    sh["iW2p"] = W2p
    sh["ib2c"] = np.ascontiguousarray(np.asarray(ins["L0_ffn_b2"], dtype=np.float32).reshape(L, 8, 128).transpose(0, 2, 1))
    sh["ilng"] = np.ascontiguousarray(np.asarray(ins["L0_ffn_ln_g"], dtype=np.float32).reshape(L, 8, 128).transpose(0, 2, 1))
    sh["ilnb"] = np.ascontiguousarray(np.asarray(ins["L0_ffn_ln_b"], dtype=np.float32).reshape(L, 8, 128).transpose(0, 2, 1))
    qs = np.asarray(ins["L0_ffn_q_scale"], dtype=np.float32)
    ks = np.asarray(ins["L0_ffn_k_scale"], dtype=np.float32)
    qsd = np.concatenate([qs, qs], axis=1)
    ksd = np.concatenate([ks, ks], axis=1)
    swap = (np.arange(128) ^ 32)
    sh["iqs"] = np.ascontiguousarray(qsd.T)
    sh["iks"] = np.ascontiguousarray(ksd.T)
    sh["iqssw"] = np.ascontiguousarray(qsd[:, swap].T)
    sh["ikssw"] = np.ascontiguousarray(ksd[:, swap].T)
    sh.update(host_consts(T))
    maps = []
    for cid in range(n_cores):
        xc = x[cid * bpc:(cid + 1) * bpc]
        m = dict(sh)
        m["xT"] = np.ascontiguousarray(xc.transpose(2, 0, 1).reshape(D, bpc * T))
        maps.append(m)
    return maps



def sview(ap, c0, c1):
    return ap.rearrange("(kb p) t -> p kb t", p=128)[:, :, c0:c1]


def wview(ap):
    return ap.rearrange("(kb p) m -> p kb m", p=128)


def r3(t, n):
    return t.ap().rearrange("p (a b) -> p a b", a=n)



def emit_ln(bc, lnb, ysrc3, rdkeys, out3, outw):
    ybf, rows = lnb["ybf"], lnb["rows"]
    pln, pl2, pbc0, pbc1 = lnb["pln"], lnb["pl2"], lnb["pbc0"], lnb["pbc1"]
    ones1, onesr = lnb["ones1"], lnb["onesr"]
    yb3 = r3(ybf, KB)
    bc.op("v", lambda e: e.tensor_copy(yb3, ysrc3), reads=rdkeys, writes=[ybf])

    def s1(e):
        for kb in range(KB):
            inst = e.matmul(pln[0:1, :], ones1[:, :], yb3[:, kb, :], start=(kb == 0), stop=(kb == KB - 1))
        return inst
    bc.op("t", s1, reads=[ybf, ones1], writes=[pln])
    bc.op("v", lambda e: e.tensor_tensor(yb3, yb3, yb3, AL.mult), reads=[ybf], writes=[ybf])

    def s2(e):
        for kb in range(KB):
            inst = e.matmul(pl2[0:1, :], ones1[:, :], yb3[:, kb, :], start=(kb == 0), stop=(kb == KB - 1))
        return inst
    bc.op("t", s2, reads=[ybf, ones1], writes=[pl2])
    MU, VA, M2 = 0, 1, 2
    def rsl(k):
        return rows[0:1, k * TT:(k + 1) * TT]
    bc.op("s", lambda e: e.activation(rsl(MU), pln[0:1, :], AF.Copy, scale=1.0 / D),
          reads=[pln], writes=[(rows.name, "mu")])
    bc.op("s", lambda e: e.activation(rsl(VA), pl2[0:1, :], AF.Copy, scale=1.0 / D),
          reads=[pl2], writes=[(rows.name, "sq")])
    bc.op("v", lambda e: e.tensor_tensor(rsl(M2), rsl(MU), rsl(MU), AL.mult),
          reads=[(rows.name, "mu")], writes=[(rows.name, "mu2")])
    bc.op("v", lambda e: e.tensor_tensor(rsl(VA), rsl(VA), rsl(M2), AL.subtract),
          reads=[(rows.name, "sq"), (rows.name, "mu2")], writes=[(rows.name, "var")])
    bc.op("s", lambda e: e.activation(rsl(VA), rsl(VA), AF.Sqrt, bias=EPS),
          reads=[(rows.name, "var"), bc.g_fc], writes=[(rows.name, "sd")])
    bc.op("v", lambda e: e.reciprocal(rsl(VA), rsl(VA)),
          reads=[(rows.name, "sd")], writes=[(rows.name, "rstd")])
    bc.op("t", lambda e: e.matmul(pbc0[:, :], onesr[:, :], rsl(MU), start=True, stop=True),
          reads=[(rows.name, "mu"), onesr], writes=[pbc0])
    bc.op("t", lambda e: e.matmul(pbc1[:, :], onesr[:, :], rsl(VA), start=True, stop=True),
          reads=[(rows.name, "rstd"), onesr], writes=[pbc1])
    mub = pbc0[:, None, :].broadcast_to([128, KB, TT])
    rsb = pbc1[:, None, :].broadcast_to([128, KB, TT])
    bc.op("v", lambda e: e.scalar_tensor_tensor(out3, ysrc3, 1.0, mub, AL.mult, AL.subtract),
          reads=rdkeys + [pbc0], writes=[outw])
    bc.op("v", lambda e: e.tensor_tensor(out3, out3, rsb, AL.mult),
          reads=[outw, pbc1], writes=[outw])


def mk_lnbufs(bc, st, pfx, nc):
    b = {}
    b["ybf"] = st.enter_context(nc.sbuf_tensor(pfx + "_lnyb", [128, KB * TT], BF16))
    b["rows"] = st.enter_context(nc.sbuf_tensor(pfx + "_lnrw", [1, 4 * TT], F32))
    b["pln"] = st.enter_context(nc.psum_tensor(pfx + "_pln", [128, 512], F32))
    b["pl2"] = st.enter_context(nc.psum_tensor(pfx + "_pl2", [128, 512], F32))
    b["pbc0"] = st.enter_context(nc.psum_tensor(pfx + "_pb0", [128, 512], F32))
    b["pbc1"] = st.enter_context(nc.psum_tensor(pfx + "_pb1", [128, 512], F32))
    b["ones1"] = bc.g_ones1
    b["onesr"] = bc.g_onesr
    return b



def sec_retnet_A(bc, cfg, dram):
    nc = bc.nc
    T, NSEQ = cfg["T"], cfg["NSEQ"]
    NT = T // TT
    with bc.section() as st:
        wq = st.enter_context(nc.sbuf_tensor("ra_wq", [128, KB * 1024], BF16))
        wk = st.enter_context(nc.sbuf_tensor("ra_wk", [128, KB * 1024], BF16))
        wv = st.enter_context(nc.sbuf_tensor("ra_wv", [128, KB * 2048], BF16))
        xbf = [st.enter_context(nc.sbuf_tensor(f"ra_x{i}", [128, KB * TT], BF16)) for i in range(2)]
        qsb = st.enter_context(nc.sbuf_tensor("ra_q", [128, KB * TT], BF16))
        ksb = st.enter_context(nc.sbuf_tensor("ra_k", [128, KB * TT], BF16))
        qrt = st.enter_context(nc.sbuf_tensor("ra_qr", [128, KB * TT], BF16))
        krt = st.enter_context(nc.sbuf_tensor("ra_kr", [128, KB * TT], BF16))
        qcd = st.enter_context(nc.sbuf_tensor("ra_qc", [128, KB * CH], BF16))
        tA = st.enter_context(nc.sbuf_tensor("ra_tA", [128, 4 * TT], BF16))
        tB = st.enter_context(nc.sbuf_tensor("ra_tB", [128, 4 * TT], BF16))
        vsb = st.enter_context(nc.sbuf_tensor("ra_v", [128, 4 * 2048], BF16))
        ktk = st.enter_context(nc.sbuf_tensor("ra_kt", [128, 2 * DKR], BF16))
        ptb = st.enter_context(nc.sbuf_tensor("ra_pt", [128, 2 * CH], BF16))
        ssb = st.enter_context(nc.sbuf_tensor("ra_S", [128, NHR * 2 * DVR], F32))
        ssbb = st.enter_context(nc.sbuf_tensor("ra_Sb", [128, NHR * 2 * DVR], BF16))
        orw = st.enter_context(nc.sbuf_tensor("ra_or", [128, 16 * TT], BF16))
        rct = [st.enter_context(nc.sbuf_tensor("ra_rc0", [128, TT], F32))]
        rst_ = [st.enter_context(nc.sbuf_tensor("ra_rs0", [128, TT], F32))]
        cdq = st.enter_context(nc.sbuf_tensor("ra_cd", [128, NHR * CH], BF16))
        mskT = st.enter_context(nc.sbuf_tensor("ra_mT", [128, NHR * 2 * CH], BF16))
        kdec = st.enter_context(nc.sbuf_tensor("ra_kd", [128, NHR * 2], F32))
        pj = [st.enter_context(nc.psum_tensor(f"ra_pj{i}", [128, 512], F32)) for i in range(2)]
        ppt = st.enter_context(nc.psum_tensor("ra_ppt", [128, 512], F32))
        pkt = st.enter_context(nc.psum_tensor("ra_pkt", [128, 1024], BF16))
        po = [st.enter_context(nc.psum_tensor(f"ra_po{i}", [128, 512], F32)) for i in range(2)]
        pd = [st.enter_context(nc.psum_tensor(f"ra_pd{i}", [128, 512], F32)) for i in range(2)]

        ident = bc.g_ident
        bc.op("g", lambda e: e.dma_start(r3(wq, KB), wview(dram["rWqT"].ap())), writes=[wq], dma=True)
        bc.op("g", lambda e: e.dma_start(r3(wk, KB), wview(dram["rWkT"].ap())), writes=[wk], dma=True)
        bc.op("g", lambda e: e.dma_start(r3(wv, KB), wview(dram["rWvT"].ap())), writes=[wv], dma=True)
        bc.op("y", lambda e: e.dma_start(
            cdq.ap().rearrange("p (h t) -> p h t", h=NHR), dram["cdq"].ap()), writes=[cdq], dma=True)
        bc.op("y", lambda e: e.dma_start(
            mskT.ap().rearrange("p (h j c) -> p h j c", h=NHR, j=2), dram["maskT"].ap()), writes=[mskT], dma=True)
        bc.op("y", lambda e: e.dma_start(
            kdec.ap().rearrange("p (h j) -> p h j", h=NHR), dram["kdec"].ap()), writes=[kdec], dma=True)

        q3, k3 = r3(qsb, KB), r3(ksb, KB)
        qr3, kr3, qc3 = r3(qrt, KB), r3(krt, KB), r3(qcd, KB)
        v3 = r3(vsb, 4)
        o3 = r3(orw, 16)
        S3 = ssb.ap().rearrange("p (h z n) -> p h z n", h=NHR, z=2)
        S3b = ssbb.ap().rearrange("p (h z n) -> p h z n", h=NHR, z=2)
        w3q, w3k, w3v = r3(wq, KB), r3(wk, KB), r3(wv, KB)
        cd3 = cdq.ap().rearrange("p (h t) -> p h t", h=NHR)
        mT3 = mskT.ap().rearrange("p (h j c) -> p h j c", h=NHR, j=2)
        kt3 = r3(ktk, 2)
        pt3 = r3(ptb, 2)
        pkt3 = r3(pkt, 2)
        ppt3 = r3(ppt, 2)

        qkeys = [(qrt.name, "ev"), (qrt.name, "od")]
        kkeys = [(krt.name, "ev"), (krt.name, "od")]

        for seq in range(NSEQ):
            bc.op("g", lambda e: e.memset(ssb[:, :], 0.0), writes=[ssb])
            bc.op("g", lambda e: e.memset(ssbb[:, :], 0.0), writes=[ssbb])
            for ti in range(NT):
                gt = seq * NT + ti
                c0 = seq * T + ti * TT
                xb = xbf[gt % 2]
                x3 = r3(xb, KB)
                bc.op("g", lambda e, x3=x3, c0=c0: e.dma_start(x3, sview(dram["xT"].ap(), c0, c0 + TT)),
                      writes=[xb], dma=True)
                rc, rs = rct[0], rst_[0]
                tc0 = ti * TT
                bc.op("y", lambda e, rc=rc, tc0=tc0: e.dma_start(rc[:, :], dram["rcos"].ap()[:, tc0:tc0 + TT]),
                      writes=[rc], dma=True)
                bc.op("y", lambda e, rs=rs, tc0=tc0: e.dma_start(rs[:, :], dram["rsin"].ap()[:, tc0:tc0 + TT]),
                      writes=[rs], dma=True)
                for wt, wbuf, dst, dname in ((w3q, wq, q3, qsb.name), (w3k, wk, k3, ksb.name)):
                    for mb in range(KB):
                        ps = pj[mb % 2]

                        def em(e, wt=wt, mb=mb, x3=x3, ps=ps):
                            for kb in range(KB):
                                inst = e.matmul(ps[:, :], wt[:, kb, mb * 128:(mb + 1) * 128], x3[:, kb, :],
                                                start=(kb == 0), stop=(kb == KB - 1))
                            return inst
                        bc.op("t", em, reads=[wbuf, xb], writes=[ps])
                        bc.op("s", lambda e, dst=dst, mb=mb, ps=ps: e.activation(dst[:, mb, :], ps[:, :], AF.Copy),
                              reads=[ps], writes=[(dname, mb)])
                for ts in range(4):
                    for nch in range(4):
                        ps = pj[(ts * 4 + nch) % 2]

                        def emv(e, ts=ts, nch=nch, x3=x3, ps=ps):
                            for kb in range(KB):
                                inst = e.matmul(ps[:, :], x3[:, kb, ts * 128:(ts + 1) * 128],
                                                w3v[:, kb, nch * 512:(nch + 1) * 512],
                                                start=(kb == 0), stop=(kb == KB - 1))
                            return inst
                        bc.op("t", emv, reads=[wv, xb], writes=[ps])
                        bc.op("s", lambda e, ts=ts, nch=nch, ps=ps: e.activation(
                            v3[:, ts, nch * 512:(nch + 1) * 512], ps[:, :], AF.Copy),
                            reads=[ps], writes=[(vsb.name, ts * 4 + nch)])
                for src, sname, dst, dkeys in ((q3, qsb.name, qr3, qkeys), (k3, ksb.name, kr3, kkeys)):
                    skeys = [(sname, mb) for mb in range(KB)]
                    a4 = r3(tA, 4)
                    b4 = r3(tB, 4)
                    ev = src[:, 0:8:2, :]
                    od = src[:, 1:8:2, :]
                    rcb = rc[:, None, :].broadcast_to([128, 4, TT])
                    rsb = rs[:, None, :].broadcast_to([128, 4, TT])
                    bc.op("v", lambda e, a4=a4, ev=ev, rcb=rcb: e.tensor_tensor(a4, ev, rcb, AL.mult),
                          reads=skeys + [rc], writes=[tA])
                    bc.op("v", lambda e, b4=b4, od=od, rsb=rsb: e.tensor_tensor(b4, od, rsb, AL.mult),
                          reads=skeys + [rs], writes=[tB])
                    bc.op("v", lambda e, dst=dst, a4=a4, b4=b4: e.tensor_tensor(dst[:, 0:8:2, :], a4, b4, AL.subtract),
                          reads=[tA, tB], writes=[dkeys[0]])
                    bc.op("v", lambda e, a4=a4, od=od, rcb=rcb: e.tensor_tensor(a4, od, rcb, AL.mult),
                          reads=skeys + [rc], writes=[tA])
                    bc.op("v", lambda e, b4=b4, ev=ev, rsb=rsb: e.tensor_tensor(b4, ev, rsb, AL.mult),
                          reads=skeys + [rs], writes=[tB])
                    bc.op("v", lambda e, dst=dst, a4=a4, b4=b4: e.tensor_tensor(dst[:, 1:8:2, :], a4, b4, AL.add),
                          reads=[tA, tB], writes=[dkeys[1]])
                qrh = qrt.ap().rearrange("p (h u t) -> p h u t", h=4, u=2)
                qch = qcd.ap().rearrange("p (h u t) -> p h u t", h=4, u=2)
                for ci in range(2):
                    tcol = ci * CH
                    bc.op("v", lambda e, qch=qch, qrh=qrh, tcol=tcol: e.tensor_tensor(
                        qch, qrh[:, :, :, tcol:tcol + CH],
                        cd3[:, :, None, :].broadcast_to([128, 4, 2, CH]), AL.mult),
                        reads=qkeys + [cdq], writes=[qcd])
                    for h in range(NHR):
                        kblk = (2 * h, 2 * h + 1)

                        def em_pt(e, h=h, tcol=tcol, kblk=kblk):
                            for js in range(2):
                                for z in range(2):
                                    inst = e.matmul(ppt3[:, js, :],
                                                    kr3[:, kblk[z], tcol + js * 128:tcol + (js + 1) * 128],
                                                    qr3[:, kblk[z], tcol:tcol + CH],
                                                    start=(js == 0 and z == 0), stop=(js == 1 and z == 1))
                            return inst
                        bc.op("t", em_pt, reads=qkeys + kkeys, writes=[ppt])
                        bc.op("v", lambda e, h=h: e.tensor_tensor(pt3, ppt3, mT3[:, h, :, :], AL.mult),
                              reads=[ppt, mskT], writes=[ptb])

                        def em_kt(e, h=h, tcol=tcol, kblk=kblk):
                            for js in range(2):
                                for z in range(2):
                                    inst = e.transpose(pkt3[:, js, z * 128:(z + 1) * 128],
                                                       kr3[:, kblk[z], tcol + js * 128:tcol + (js + 1) * 128],
                                                       ident[:, :])
                            return inst
                        bc.op("t", em_kt, reads=kkeys + [ident], writes=[pkt])
                        for js in range(2):
                            bc.op("s", lambda e, js=js, h=h: e.activation(
                                kt3[:, js, :], pkt3[:, js, 0:256], AF.Copy,
                                scale=kdec.ap().rearrange("p (h j) -> p h j", h=NHR)[:, h, js:js + 1]),
                                reads=[pkt, kdec], writes=[(ktk.name, js)])

                        def em_o(e, h=h, ci=ci, tcol=tcol, kblk=kblk):
                            for bank in range(2):
                                first = True
                                for s_ in range(2):
                                    dvs = bank * 2 + s_
                                    col = slice(s_ * CH, (s_ + 1) * CH)
                                    for js in range(2):
                                        e.matmul(po[bank][:, col],
                                                 v3[:, ci * 2 + js, h * 512 + dvs * 128:h * 512 + (dvs + 1) * 128],
                                                 pt3[:, js, :], start=first, stop=False)
                                        first = False
                                    for z in range(2):
                                        last = (s_ == 1 and z == 1)
                                        inst = e.matmul(po[bank][:, col],
                                                        S3b[:, h, z, dvs * 128:(dvs + 1) * 128],
                                                        qc3[:, kblk[z], :],
                                                        start=False, stop=last)
                            return inst
                        bc.op("t", em_o, reads=[vsb, ptb, ssbb, qcd], writes=[po[0], po[1]])
                        for bank in range(2):
                            bc.op("s", lambda e, bank=bank, h=h, tcol=tcol: e.activation(
                                o3[:, 4 * h + 2 * bank:4 * h + 2 * bank + 2, tcol:tcol + CH],
                                po[bank].ap().rearrange("p (s c) -> p s c", s=2), AF.Copy),
                                reads=[po[bank]], writes=[(orw.name, (h, bank, ci))])

                        def em_d(e, h=h, ci=ci):
                            for z in range(2):
                                for js in range(2):
                                    inst = e.matmul(pd[z][:, :], kt3[:, js, z * 128:(z + 1) * 128],
                                                    v3[:, ci * 2 + js, h * 512:h * 512 + 512],
                                                    start=(js == 0), stop=(js == 1))
                            return inst
                        bc.op("t", em_d, reads=[(ktk.name, 0), (ktk.name, 1), vsb], writes=[pd[0], pd[1]])
                        gC = float((1.0 - 2.0 ** (-5.0 - h)) ** CH)
                        for z in range(2):
                            bc.op("v", lambda e, z=z, h=h, gC=gC: e.scalar_tensor_tensor(
                                S3[:, h, z, :], S3[:, h, z, :], gC, pd[z][:, :], AL.mult, AL.add),
                                reads=[pd[z], ssb], writes=[ssb])
                            bc.op("v", lambda e, z=z, h=h: e.tensor_copy(S3b[:, h, z, :], S3[:, h, z, :]),
                                  reads=[ssb], writes=[ssbb])
                okeys = [(orw.name, (h, bank, ci)) for h in range(4) for bank in range(2) for ci in range(2)]
                bc.op("s", lambda e, o3=o3, c0=c0: e.dma_start(sview(dram["orawT"].ap(), c0, c0 + TT), o3),
                      reads=okeys, dma=True)


def sec_retnet_BC(bc, cfg, dram):
    nc = bc.nc
    T, NSEQ = cfg["T"], cfg["NSEQ"]
    NT = T // TT
    with bc.section() as st:
        wg = st.enter_context(nc.sbuf_tensor("rb_wg", [128, KB * 2048], BF16))
        wo = st.enter_context(nc.sbuf_tensor("rb_wo", [128, 16 * 1024], BF16))
        stage = [st.enter_context(nc.sbuf_tensor("rb_st0", [128, KB * TT], F32))]
        xbf = st.enter_context(nc.sbuf_tensor("rb_xb", [128, KB * TT], BF16))
        gsb = st.enter_context(nc.sbuf_tensor("rb_g", [128, 16 * TT], BF16))
        osb = [st.enter_context(nc.sbuf_tensor("rb_o0", [128, 16 * TT], BF16))]
        ofn = st.enter_context(nc.sbuf_tensor("rb_of", [128, 16 * TT], BF16))
        ysb = st.enter_context(nc.sbuf_tensor("rb_y", [128, KB * TT], F32))
        grow = st.enter_context(nc.sbuf_tensor("rb_gr", [4, TT], F32))
        rsbb = st.enter_context(nc.sbuf_tensor("rb_rb", [128, 4 * TT], BF16))
        gnw = st.enter_context(nc.sbuf_tensor("rb_gn", [128, 4], F32))
        zs7 = st.enter_context(nc.sbuf_tensor("rb_z7", [128, 7], BF16))
        zs7t = st.enter_context(nc.sbuf_tensor("rb_zt", [4, 4 * 128], F32))
        pq = [st.enter_context(nc.psum_tensor(f"rb_pq{i}", [128, 512], F32)) for i in range(2)]
        pst_ = st.enter_context(nc.psum_tensor("rb_ps", [128, 512], F32))
        prs = st.enter_context(nc.psum_tensor("rb_pr", [128, 512], F32))
        lnb = mk_lnbufs(bc, st, "rb", nc)

        bc.op("g", lambda e: e.dma_start(r3(wg, KB), wview(dram["rWgT"].ap())), writes=[wg], dma=True)
        bc.op("g", lambda e: e.dma_start(r3(wo, 16), wview(dram["rWoT"].ap())), writes=[wo], dma=True)
        bc.op("y", lambda e: e.dma_start(gnw[:, :], dram["gnw4"].ap()), writes=[gnw], dma=True)
        bc.op("y", lambda e: e.dma_start(zs7[:, :], dram["z7"].ap()), writes=[zs7], dma=True)
        bc.op("y", lambda e: e.dma_start(zs7t[:, :], dram["z7t"].ap()), writes=[zs7t], dma=True)
        w3o = r3(wo, 16)
        w3g = r3(wg, KB)
        for kb in range(16):
            bc.op("v", lambda e, kb=kb: e.tensor_scalar_mul(w3o[:, kb, :], w3o[:, kb, :],
                                                            gnw[:, kb % 4:kb % 4 + 1]),
                  reads=[wo, gnw], writes=[wo])
        g3 = r3(gsb, 16)
        of3 = r3(ofn, 16)
        oq3 = of3
        for seq in range(NSEQ):
            for ti in range(NT):
                gt = seq * NT + ti
                c0 = seq * T + ti * TT
                stg = stage[0]
                osl = osb[0]
                st3 = r3(stg, KB)
                o3 = r3(osl, 16)
                x3 = r3(xbf, KB)
                bc.op("y", lambda e, st3=st3, c0=c0: e.dma_start(st3, sview(dram["xT"].ap(), c0, c0 + TT)),
                      writes=[stg], dma=True)
                bc.op("g", lambda e, o3=o3, c0=c0: e.dma_start(o3, sview(dram["orawT"].ap(), c0, c0 + TT)),
                      writes=[osl], dma=True)
                bc.op("v", lambda e, x3=x3, st3=st3: e.tensor_copy(x3, st3), reads=[stg], writes=[xbf])
                for mb in range(16):
                    ps = pq[mb % 2]

                    def emg(e, mb=mb, x3=x3, ps=ps):
                        for kb in range(KB):
                            inst = e.matmul(ps[:, :], w3g[:, kb, mb * 128:(mb + 1) * 128], x3[:, kb, :],
                                            start=(kb == 0), stop=(kb == KB - 1))
                        return inst
                    bc.op("t", emg, reads=[wg, xbf], writes=[ps])
                    bc.op("s", lambda e, mb=mb, ps=ps: e.activation(g3[:, mb, :], ps[:, :], AF.Silu),
                          reads=[ps], writes=[(gsb.name, mb)])
                bc.op("v", lambda e, o3=o3: e.tensor_tensor(oq3, o3, o3, AL.mult),
                      reads=[osl], writes=[(ofn.name, "sq")])

                def em_ss(e):
                    for b in range(16):
                        jj = b // 4
                        inst = e.matmul(pst_[0:4, :], zs7[:, 3 - jj:7 - jj], oq3[:, b, :],
                                        start=(b == 0), stop=(b == 15))
                    return inst
                bc.op("t", em_ss, reads=[(ofn.name, "sq"), zs7], writes=[pst_])
                bc.op("s", lambda e: e.activation(grow[0:4, 0:TT], pst_[0:4, :], AF.Sqrt,
                                                  bias=EPS, scale=1.0 / DVR),
                      reads=[pst_, bc.g_fc], writes=[(grow.name, "sd")])
                bc.op("v", lambda e: e.reciprocal(grow[0:4, 0:TT], grow[0:4, 0:TT]),
                      reads=[(grow.name, "sd")], writes=[(grow.name, "r")])
                for h in range(4):
                    bc.op("t", lambda e, h=h: e.matmul(prs[:, :], zs7t[:, h * 128:(h + 1) * 128], grow[0:4, 0:TT],
                                                       start=True, stop=True),
                          reads=[(grow.name, "r"), zs7t], writes=[prs])
                    bc.op("s", lambda e, h=h: e.activation(rsbb[:, h * TT:(h + 1) * TT], prs[:, :], AF.Copy),
                          reads=[prs], writes=[(rsbb.name, h)])
                bc.op("v", lambda e, o3=o3: e.tensor_tensor(of3, o3, g3, AL.mult),
                      reads=[osl] + [(gsb.name, mb) for mb in range(16)], writes=[(ofn.name, "t1")])
                rs4 = rsbb.ap().rearrange("p (h t) -> p h t", h=4)
                ofh = ofn.ap().rearrange("p (h u t) -> p h u t", h=4, u=4)
                bc.op("v", lambda e, ofh=ofh: e.tensor_tensor(
                    ofh, ofh, rs4[:, :, None, :].broadcast_to([128, 4, 4, TT]), AL.mult),
                    reads=[(ofn.name, "t1")] + [(rsbb.name, h) for h in range(4)], writes=[ofn])
                for mb in range(KB):
                    ps = pq[mb % 2]

                    def emo(e, mb=mb, ps=ps):
                        for kb in range(16):
                            inst = e.matmul(ps[:, :], w3o[:, kb, mb * 128:(mb + 1) * 128], of3[:, kb, :],
                                            start=(kb == 0), stop=(kb == 15))
                        return inst
                    bc.op("t", emo, reads=[wo, ofn], writes=[ps])
                    bc.op("v", lambda e, mb=mb, ps=ps, st3=st3: e.tensor_tensor(
                        r3(ysb, KB)[:, mb, :], ps[:, :], st3[:, mb, :], AL.add),
                        reads=[ps, stg], writes=[(ysb.name, mb)])
                emit_ln(bc, lnb, r3(ysb, KB), [(ysb.name, mb) for mb in range(KB)], r3(ysb, KB), ysb)
                bc.op("s", lambda e, c0=c0: e.dma_start(sview(dram["xln1T"].ap(), c0, c0 + TT), r3(ysb, KB)),
                      reads=[ysb], dma=True)


def sec_final(bc, cfg, dram, src):
    nc = bc.nc
    T, NSEQ = cfg["T"], cfg["NSEQ"]
    NT = T // TT
    with bc.section() as st:
        sa = [st.enter_context(nc.sbuf_tensor(f"fn_a{i}", [128, KB * TT], F32)) for i in range(2)]
        sb_ = [st.enter_context(nc.sbuf_tensor(f"fn_b{i}", [128, KB * TT], F32)) for i in range(2)]
        ysb = st.enter_context(nc.sbuf_tensor("fn_y", [128, KB * TT], F32))
        lnb = mk_lnbufs(bc, st, "fn", nc)
        for seq in range(NSEQ):
            for ti in range(NT):
                gt = seq * NT + ti
                c0 = seq * T + ti * TT
                a, b = sa[gt % 2], sb_[gt % 2]
                a3, b3 = r3(a, KB), r3(b, KB)
                bc.op("y", lambda e, a3=a3, c0=c0: e.dma_start(a3, sview(dram["xln1T"].ap(), c0, c0 + TT)),
                      writes=[a], dma=True)
                bc.op("g", lambda e, b3=b3, c0=c0: e.dma_start(b3, sview(dram[src].ap(), c0, c0 + TT)),
                      writes=[b], dma=True)
                bc.op("v", lambda e, a3=a3, b3=b3: e.tensor_tensor(r3(ysb, KB), a3, b3, AL.add),
                      reads=[a, b], writes=[ysb])
                emit_ln(bc, lnb, r3(ysb, KB), [ysb], r3(ysb, KB), ysb)
                bc.op("s", lambda e, c0=c0: e.dma_start(sview(dram["outT"].ap(), c0, c0 + TT), r3(ysb, KB)),
                      reads=[ysb], dma=True)



def build_program(cfg):
    nc = bass.Bass("TRN2", target_bir_lowering=False)
    T, NSEQ, nlayer = cfg["T"], cfg["NSEQ"], cfg["nlayer"]
    TC = T * NSEQ
    dram = {}
    dram["xT"] = nc.dram_tensor("xT", [D, TC], F32, kind="ExternalInput")
    for n, shp, dt in [("rWqT", [D, 1024], F32), ("rWkT", [D, 1024], F32),
                       ("rWvT", [D, 2048], F32), ("rWgT", [D, 2048], F32),
                       ("rWoT", [2048, 1024], F32), ("gnw4", [128, 4], F32),
                       ("iWqkv", [6, D, 3072], F32), ("iWo", [6, D, 1024], F32),
                       ("iW1p", [6, D, 5632], F32), ("ib1p", [6, 128, 2, 22], F32),
                       ("iW2p", [6, 2816, D], F32), ("ib2c", [6, 128, 8], F32),
                       ("ilng", [6, 128, 8], F32), ("ilnb", [6, 128, 8], F32),
                       ("iqs", [128, 6], F32), ("iks", [128, 6], F32),
                       ("iqssw", [128, 6], F32), ("ikssw", [128, 6], F32),
                       ("ident", [128, 128], BF16), ("ident32", [128, 128], F32),
                       ("rcos", [128, T], F32), ("rsin", [128, T], F32),
                       ("maskT", [128, NHR, 2, CH], BF16), ("cdq", [128, NHR, CH], BF16),
                       ("kdec", [128, NHR, 2], F32),
                       ("iqa", [128, T], F32), ("iqb", [128, T], F32),
                       ("ika", [128, T], F32), ("ikb", [128, T], F32),
                       ("imask", [128, 4, 512], BF16), ("zl2", [128, 33], BF16),
                       ("bclhs", [16, KB * 128], BF16), ("z7", [128, 7], BF16),
                       ("z7t", [4, 4 * 128], F32), ("onescol", [128, 1], BF16),
                       ("onesrow32", [1, 128], F32), ("fconsts", [128, 3], F32)]:
        dram[n] = nc.dram_tensor(n, shp, dt, kind="ExternalInput")
    okind = "ExternalOutput" if cfg.get("dbg") else "Internal"
    dram["wqkvC"] = nc.dram_tensor("wqkvC", [6, D, 3072], BF16, kind="Internal")
    dram["woC"] = nc.dram_tensor("woC", [6, D, 1024], BF16, kind="Internal")
    dram["orawT"] = nc.dram_tensor("orawT", [2048, TC], BF16, kind=okind)
    dram["xln1T"] = nc.dram_tensor("xln1T", [D, TC], F32, kind=okind)
    dram["yA"] = nc.dram_tensor("yA", [D, TC], F32, kind=okind)
    dram["yB"] = nc.dram_tensor("yB", [D, TC], F32, kind=okind)
    dram["outT"] = nc.dram_tensor("outT", [D, TC], F32, kind="ExternalOutput")
    if cfg.get("dbg"):
        dram["qkdbgT"] = nc.dram_tensor("qkdbgT", [2048, TC], BF16, kind="ExternalOutput")
        dram["vdbgT"] = nc.dram_tensor("vdbgT", [TC, 1024], BF16, kind="ExternalOutput")
        dram["odbgT"] = nc.dram_tensor("odbgT", [D, TC], BF16, kind="ExternalOutput")

    bc = BC(nc)
    bc.g_ident = bc.es.enter_context(nc.sbuf_tensor("g_ident", [128, 128], BF16))
    bc.g_ident32 = bc.es.enter_context(nc.sbuf_tensor("g_ident32", [128, 128], F32))
    bc.g_ones1 = bc.es.enter_context(nc.sbuf_tensor("g_ones1", [128, 1], BF16))
    bc.g_onesr = bc.es.enter_context(nc.sbuf_tensor("g_onesr", [1, 128], F32))

    bc.g_imask = bc.es.enter_context(nc.sbuf_tensor("g_imask", [128, 4 * 512], BF16))
    bc.g_zl2 = bc.es.enter_context(nc.sbuf_tensor("g_zl2", [128, 33], BF16))
    bc.g_qs = bc.es.enter_context(nc.sbuf_tensor("g_qs", [128, 6], F32))
    bc.g_ks = bc.es.enter_context(nc.sbuf_tensor("g_ks", [128, 6], F32))
    bc.g_qssw = bc.es.enter_context(nc.sbuf_tensor("g_qssw", [128, 6], F32))
    bc.g_kssw = bc.es.enter_context(nc.sbuf_tensor("g_kssw", [128, 6], F32))
    bc.g_lng = bc.es.enter_context(nc.sbuf_tensor("g_lng", [128, 6 * 8], F32))
    bc.g_lnb = bc.es.enter_context(nc.sbuf_tensor("g_lnb", [128, 6 * 8], F32))
    bc.g_b2 = bc.es.enter_context(nc.sbuf_tensor("g_b2", [128, 6 * 8], F32))
    bc.g_fc = bc.es.enter_context(nc.sbuf_tensor("g_fc", [128, 3], F32))
    bc.op("y", lambda e: e.dma_start(bc.g_fc[:, :], dram["fconsts"].ap()), writes=[bc.g_fc], dma=True)
    nc.const_aps.register(F32, EPS, bc.g_fc[:, 0:1])
    nc.const_aps.register(F32, 1e-24, bc.g_fc[:, 1:2])
    nc.const_aps.register(F32, float(np.log(8.0)), bc.g_fc[:, 2:3])
    bc.op("y", lambda e: e.dma_start(bc.g_ident[:, :], dram["ident"].ap()), writes=[bc.g_ident], dma=True)
    bc.op("y", lambda e: e.dma_start(bc.g_ident32[:, :], dram["ident32"].ap()), writes=[bc.g_ident32], dma=True)
    bc.op("y", lambda e: e.dma_start(bc.g_ones1[:, :], dram["onescol"].ap()), writes=[bc.g_ones1], dma=True)
    bc.op("y", lambda e: e.dma_start(bc.g_onesr[:, :], dram["onesrow32"].ap()), writes=[bc.g_onesr], dma=True)

    bc.op("y", lambda e: e.dma_start(r3(bc.g_imask, 4), dram["imask"].ap()), writes=[bc.g_imask], dma=True)
    bc.op("y", lambda e: e.dma_start(bc.g_zl2[:, :], dram["zl2"].ap()), writes=[bc.g_zl2], dma=True)
    for buf, nm in ((bc.g_qs, "iqs"), (bc.g_ks, "iks"), (bc.g_qssw, "iqssw"), (bc.g_kssw, "ikssw")):
        bc.op("y", lambda e, buf=buf, nm=nm: e.dma_start(buf[:, :], dram[nm].ap()), writes=[buf], dma=True)
    for buf, nm in ((bc.g_lng, "ilng"), (bc.g_lnb, "ilnb"), (bc.g_b2, "ib2c")):
        bc.op("y", lambda e, buf=buf, nm=nm: e.dma_start(
            buf.ap().rearrange("p (l c) -> p l c", l=6),
            dram[nm].ap().rearrange("l p c -> p l c")), writes=[buf], dma=True)

    bc.op("g", lambda e: e.dma_start(dram["wqkvC"].ap(), dram["iWqkv"].ap()),
          writes=["wqkvC"], dma=True)
    bc.op("g", lambda e: e.dma_start(dram["woC"].ap(), dram["iWo"].ap()),
          writes=["woC"], dma=True)
    sec_retnet_A(bc, cfg, dram)
    sec_retnet_BC(bc, cfg, dram)
    cur = "xln1T"
    for l in range(nlayer):
        nxt = "yA" if cur in ("xln1T", "yB") else "yB"
        sec_attn(bc, cfg, dram, l, cur, nxt)
        if cfg.get("attn_only"):
            cur = nxt
            break
        cur2 = "yB" if nxt == "yA" else "yA"
        sec_ffn(bc, cfg, dram, l, nxt, cur2)
        cur = cur2
    if not cfg.get("attn_only"):
        sec_final(bc, cfg, dram, cur)
    bc.flush()
    return nc


def sec_attn(bc, cfg, dram, l, src, dst):
    nc = bc.nc
    T, NSEQ = cfg["T"], cfg["NSEQ"]
    NT = T // TT
    NTILE = NSEQ * NT
    with bc.section() as st:
        wqkv = st.enter_context(nc.sbuf_tensor(f"at{l}_wq", [128, KB * 3072], BF16))
        wch = st.enter_context(nc.sbuf_tensor(f"at{l}_wc", [128, 4 * KB * 128], BF16))
        bcl = st.enter_context(nc.sbuf_tensor(f"at{l}_bc", [16, KB * 128], BF16))
        stage = [st.enter_context(nc.sbuf_tensor(f"at{l}_st{i}", [128, KB * (TT + 1)], F32)) for i in range(2)]
        xsh = [st.enter_context(nc.sbuf_tensor(f"at{l}_xs", [128, KB * TT], BF16))]
        qb = [st.enter_context(nc.sbuf_tensor(f"at{l}_qb{i}", [128, KB * TT], BF16)) for i in range(2)]
        kb_ = st.enter_context(nc.sbuf_tensor(f"at{l}_kb", [128, KB * TT], BF16))
        sqb = st.enter_context(nc.sbuf_tensor(f"at{l}_sq", [128, KB * TT], BF16))
        tabs = [st.enter_context(nc.sbuf_tensor(f"at{l}_tb", [128, 2 * TT], F32))]
        rows = [st.enter_context(nc.sbuf_tensor(f"at{l}_rw{i}", [16, 2 * TT], BF16)) for i in range(2)]
        rowf = st.enter_context(nc.sbuf_tensor(f"at{l}_rwf", [16, TT], F32))
        kring = st.enter_context(nc.sbuf_tensor(f"at{l}_kr", [128, KB * 3 * 256], BF16))
        vring = st.enter_context(nc.sbuf_tensor(f"at{l}_vr", [128, 10 * 1024], BF16))
        psb = st.enter_context(nc.sbuf_tensor(f"at{l}_P", [128, 2 * 3 * 512], BF16))
        lrf = st.enter_context(nc.sbuf_tensor(f"at{l}_lf", [1, 2 * 512], F32))
        osb = st.enter_context(nc.sbuf_tensor(f"at{l}_O", [128, KB * TT], BF16))
        pS = [st.enter_context(nc.psum_tensor(f"at{l}_pS{i}", [128, 512], F32)) for i in range(4)]
        pw = [st.enter_context(nc.psum_tensor(f"at{l}_pw{i}", [128, 512], F32)) for i in range(2)]
        prl = [st.enter_context(nc.psum_tensor(f"at{l}_pr{i}", [128, 512], F32)) for i in range(1)]
        pJ = st.enter_context(nc.psum_tensor(f"at{l}_pJ", [128, 512], F32))

        ident = bc.g_ident
        im3 = r3(bc.g_imask, 4)
        sq3 = r3(sqb, KB)
        kr4 = kring.ap().rearrange("p (b w c) -> p b w c", b=KB, w=3)
        w3q = r3(wqkv, KB)
        vr3 = r3(vring, 10)
        o3w = osb.ap().rearrange("p (b t) -> p b t", b=KB)
        wqC = dram["wqkvC"].ap()[l].rearrange("(kb p) m -> p kb m", p=128)
        woC = dram["woC"].ap()[l].rearrange("(kb p) m -> p kb m", p=128)

        bc.op("g", lambda e: e.dma_start(r3(wqkv, KB), wqC), reads=["wqkvC"], writes=[wqkv], dma=True)
        bc.op("y", lambda e: e.dma_start(bcl[:, :], dram["bclhs"].ap()), writes=[bcl], dma=True)
        bc.op("g", lambda e: e.memset(kring[:, :], 0.0), writes=[kring])
        bc.op("g", lambda e: e.memset(vring[:, :], 0.0), writes=[vring])

        def make_tile_ops(gt):
            seq, ti = divmod(gt, NT)
            c0 = seq * T + ti * TT
            par = gt % 2
            stg = stage[par]
            st3 = stg.ap().rearrange("p (kb t) -> p kb t", kb=KB)
            x3s = xsh[0].ap().rearrange("p (kb t) -> p kb t", kb=KB)
            q3b = r3(qb[par], KB)
            k3b = r3(kb_, KB)
            tb3 = r3(tabs[0], 2)
            rw = rows[par]
            ops = []
            if ti == 0:
                ops.append(lambda: bc.op("y", lambda e: e.dma_start(
                    st3[:, :, 1:TT + 1], sview(dram[src].ap(), c0, c0 + TT)), writes=[stg], dma=True))
                ops.append(lambda: bc.op("g", lambda e: e.memset(st3[:, :, 0:1], 0.0),
                                         writes=[(stg.name, "z")]))
            else:
                ops.append(lambda: bc.op("y", lambda e: e.dma_start(
                    st3, sview(dram[src].ap(), c0 - 1, c0 + TT)), writes=[stg], dma=True))
            stkeys = [stg, (stg.name, "z")] if ti == 0 else [stg]
            ops.append(lambda: bc.op("v", lambda e: e.tensor_copy(x3s[:, 0:4, :], st3[:, 0:4, 1:TT + 1]),
                                     reads=stkeys, writes=[(xsh[0].name, 0)]))
            ops.append(lambda: bc.op("v", lambda e: e.tensor_copy(x3s[:, 4:8, :], st3[:, 4:8, 0:TT]),
                                     reads=stkeys, writes=[(xsh[0].name, 1)]))
            xkeys = [(xsh[0].name, 0), (xsh[0].name, 1)]
            for mb in range(16):
                def ldmm(mb=mb, x3s=x3s, q3b=q3b, k3b=k3b, par=par):
                    def em(e, mb=mb, x3s=x3s):
                        for kb in range(KB):
                            inst = e.matmul(pJ[:, :], w3q[:, kb, mb * 128:(mb + 1) * 128], x3s[:, kb, :],
                                            start=(kb == 0), stop=(kb == KB - 1))
                        return inst
                    bc.op("t", em, reads=[wqkv] + xkeys, writes=[pJ])
                    if mb < 8:
                        bc.op("s", lambda e, mb=mb, q3b=q3b: e.activation(q3b[:, mb, :], pJ[:, :], AF.Copy),
                              reads=[pJ], writes=[(qb[par].name, mb)])
                    else:
                        bc.op("s", lambda e, mb=mb, k3b=k3b: e.activation(k3b[:, mb - 8, :], pJ[:, :], AF.Copy),
                              reads=[pJ], writes=[(kb_.name, mb - 8)])
                ops.append(ldmm)
            for nch in range(2):
                for ts_ in range(4):
                    sv = (gt * 4 + ts_) % 10

                    def mmv(nch=nch, ts_=ts_, sv=sv, x3s=x3s):
                        def em(e, nch=nch, ts_=ts_, x3s=x3s):
                            for kb in range(KB):
                                inst = e.matmul(pJ[:, :], x3s[:, kb, ts_ * 128:(ts_ + 1) * 128],
                                                w3q[:, kb, 2048 + nch * 512:2048 + (nch + 1) * 512],
                                                start=(kb == 0), stop=(kb == KB - 1))
                            return inst
                        bc.op("t", em, reads=[wqkv] + xkeys, writes=[pJ])
                        bc.op("s", lambda e, sv=sv, nch=nch: e.activation(
                            vr3[:, sv, nch * 512:(nch + 1) * 512], pJ[:, :], AF.Copy),
                            reads=[pJ], writes=[(vring.name, sv * 2 + nch)])
                    ops.append(mmv)
            tc0 = ti * TT
            for half, base in ((0, 0), (1, 8)):
                hb = q3b if half == 0 else k3b
                hbuf = qb[par] if half == 0 else kb_
                hkeys = [(hbuf.name, b) for b in range(8)]
                rcol = half * TT

                def sq_ss(half=half, rcol=rcol, hb=hb, hkeys=hkeys, rw=rw, par=par):
                    bc.op("v", lambda e, hb=hb: e.tensor_tensor(
                        sq3, hb, hb, AL.mult),
                        reads=hkeys, writes=[sqb])

                    def emss(e):
                        for b in range(KB):
                            inst = e.matmul(pJ[0:16, :], bc.g_zl2[:, 16 - 2 * b:32 - 2 * b], sq3[:, b, :],
                                            start=(b == 0), stop=(b == KB - 1))
                        return inst
                    bc.op("t", emss, reads=[sqb, bc.g_zl2], writes=[pJ])
                    scl = (1.0 / 64.0) if half == 0 else 1.0
                    flo = 1.25e-13 if half == 0 else 1e-12
                    bc.op("s", lambda e, scl=scl: e.activation(
                        rowf[0:16, :], pJ[0:16, :], AF.Sqrt, scale=scl),
                        reads=[pJ, (rowf.name, "c")], writes=[(rowf.name, "sd")])
                    bc.op("v", lambda e, flo=flo: e.tensor_scalar_max(
                        rowf[0:16, :], rowf[0:16, :], flo),
                        reads=[(rowf.name, "sd")], writes=[(rowf.name, "sd")])
                    bc.op("v", lambda e: e.reciprocal(
                        rowf[0:16, :], rowf[0:16, :]),
                        reads=[(rowf.name, "sd")], writes=[(rowf.name, "r")])
                    bc.op("v", lambda e, rcol=rcol, rw=rw: e.tensor_copy(
                        rw[0:16, rcol:rcol + TT], rowf[0:16, :]),
                        reads=[(rowf.name, "r")], writes=[(rw.name, half), (rowf.name, "c")])
                ops.append(sq_ss)
                nms = ("iqa", "iqb") if half == 0 else ("ika", "ikb")
                scs = (bc.g_qs, bc.g_qssw) if half == 0 else (bc.g_ks, bc.g_kssw)
                for k_, (nm, sc) in enumerate(zip(nms, scs)):
                    def tabld(k_=k_, nm=nm, sc=sc, tb3=tb3):
                        bc.op("y", lambda e, k_=k_, nm=nm, tb3=tb3: e.dma_start(
                            tb3[:, k_, :], dram[nm].ap()[:, tc0:tc0 + TT]),
                            writes=[(tabs[0].name, k_)], dma=True)
                        bc.op("v", lambda e, k_=k_, sc=sc, tb3=tb3: e.tensor_scalar_mul(
                            tb3[:, k_, :], tb3[:, k_, :], sc[:, l:l + 1]),
                            reads=[(tabs[0].name, k_), sc], writes=[(tabs[0].name, k_)])
                    ops.append(tabld)

                def rot(half=half, hb=hb, hkeys=hkeys, tb3=tb3):
                    for qn, (o0, i0) in enumerate(((0, 32), (32, 0), (64, 96), (96, 64))):
                        bc.op("y", lambda e, o0=o0, i0=i0, hb=hb: e.dma_start(
                            sq3[o0:o0 + 32, :, :], hb[i0:i0 + 32, :, :]),
                            reads=hkeys, writes=[(sqb.name, qn)], dma=True)
                    swkeys = [(sqb.name, qn) for qn in range(4)]
                    tbb = tb3[:, 0, None, :].broadcast_to([128, 8, TT])
                    tbs = tb3[:, 1, None, :].broadcast_to([128, 8, TT])
                    bc.op("v", lambda e, tbb=tbb, hb=hb: e.tensor_tensor(hb, hb, tbb, AL.mult),
                        reads=hkeys + swkeys + [(tabs[0].name, 0)], writes=hkeys)
                    bc.op("v", lambda e, tbs=tbs: e.tensor_tensor(sq3, sq3, tbs, AL.mult),
                          reads=swkeys + [(tabs[0].name, 1)], writes=[sqb])
                    bc.op("v", lambda e, hb=hb: e.tensor_tensor(hb, hb, sq3, AL.add),
                        reads=hkeys + [sqb], writes=hkeys)
                ops.append(rot)
            for b in range(KB):
                def rqb(b=b, rw=rw, q3b=q3b, par=par):
                    bc.op("t", lambda e, b=b, rw=rw: e.matmul(
                        pJ[:, :], bcl[0:16, b * 128:(b + 1) * 128], rw[0:16, 0:TT],
                        start=True, stop=True),
                        reads=[(rw.name, 0), bcl], writes=[pJ])
                    bc.op("v", lambda e, b=b, q3b=q3b: e.tensor_tensor(
                        q3b[:, b, :], q3b[:, b, :], pJ[:, :], AL.mult),
                        reads=[(qb[par].name, b), pJ], writes=[(qb[par].name, b)])
                ops.append(rqb)
            for b in range(KB):
                def rkb(b=b, rw=rw, k3b=k3b):
                    bc.op("t", lambda e, b=b, rw=rw: e.matmul(
                        pJ[:, :], bcl[0:16, b * 128:(b + 1) * 128], rw[0:16, TT:2 * TT],
                        start=True, stop=True),
                        reads=[(rw.name, 1), bcl], writes=[pJ])
                    bc.op("v", lambda e, b=b, k3b=k3b: e.tensor_tensor(
                        k3b[:, b, :], k3b[:, b, :], pJ[:, :], AL.mult),
                        reads=[(kb_.name, b), pJ], writes=[(kb_.name, b)])
                ops.append(rkb)
            for wl_ in range(2):
                def kcp(wl_=wl_, k3b=k3b, ti=ti):
                    wg_ = ti * 2 + wl_
                    bc.op("v", lambda e, wl_=wl_, wg_=wg_, k3b=k3b: e.tensor_copy(
                        kr4[:, :, wg_ % 3, :], k3b[:, :, wl_ * 256:(wl_ + 1) * 256]),
                        reads=[(kb_.name, b) for b in range(8)], writes=[(kring.name, wg_ % 3)])
                ops.append(kcp)
            return ops

        def emit_win_S(gt, wl, hp):
            seq, ti = divmod(gt, NT)
            par = gt % 2
            q3b = r3(qb[par], KB)
            wg = ti * 2 + wl
            w0 = (wg == 0)
            pprev = (wg - 1) % 3
            pown = wg % 3
            par2 = hp % 2
            qc = wl * 256
            ve = 1 if w0 else 0
            vm = 3 if w0 else 2
            eb = par2 * 1536
            skeys = [(qb[par].name, hp), (kring.name, pprev), (kring.name, pown),
                     ident, bc.g_imask]
            for c in range(2):
                r0 = 64 * c

                def emE(e, c=c, r0=r0, hp=hp, qc=qc, pprev=pprev, pown=pown, w0=w0, ve=ve, q3b=q3b):
                    e.matmul(pS[c][:, 0:256], ident[:, :], im3[:, ve, 0:256],
                             start=True, stop=False)
                    if not w0:
                        e.matmul(pS[c][:, 0:128],
                                 kr4[r0:r0 + 64, hp, pprev, 0:128],
                                 q3b[r0:r0 + 64, hp, qc:qc + 128],
                                 start=False, stop=False, tile_position=(r0, 0))
                    return e.matmul(pS[c][:, 128:256],
                                    kr4[r0:r0 + 64, hp, pown, 128:256],
                                    q3b[r0:r0 + 64, hp, qc + 128:qc + 256],
                                    start=False, stop=True, tile_position=(r0, 0))
                bc.op("t", emE, reads=skeys, writes=[pS[c]])
                bc.op("s", lambda e, c=c, eb=eb: e.activation(
                    psb[:, eb + c * 256:eb + c * 256 + 256], pS[c][:, 0:256], AF.Exp),
                    reads=[pS[c]], writes=[(psb.name, par2, c)])
            for c in range(2):
                r0 = 64 * c

                def emM(e, c=c, r0=r0, hp=hp, qc=qc, pprev=pprev, pown=pown, w0=w0, vm=vm, q3b=q3b):
                    e.matmul(pS[2 + c][:, 0:512], ident[:, :], im3[:, vm, :],
                             start=True, stop=False)
                    if not w0:
                        e.matmul(pS[2 + c][:, 0:256],
                                 kr4[r0:r0 + 64, hp, pprev, 128:256],
                                 q3b[r0:r0 + 64, hp, qc:qc + 256],
                                 start=False, stop=False, tile_position=(r0, 0))
                    return e.matmul(pS[2 + c][:, 256:512],
                                    kr4[r0:r0 + 64, hp, pown, 0:128],
                                    q3b[r0:r0 + 64, hp, qc:qc + 256],
                                    start=False, stop=True, tile_position=(r0, 0))
                bc.op("t", emM, reads=skeys, writes=[pS[2 + c]])
                bc.op("s", lambda e, c=c, eb=eb: e.activation(
                    psb[:, eb + 512 + c * 512:eb + 1024 + c * 512], pS[2 + c][:, 0:512], AF.Exp),
                    reads=[pS[2 + c]], writes=[(psb.name, par2, 2 + c)])

        def emit_win_tail(gt, wl, hp):
            seq, ti = divmod(gt, NT)
            wg = ti * 2 + wl
            w0 = (wg == 0)
            par2 = hp % 2
            gw = gt * 4 + 2 * wl
            kbs = (2, 3) if w0 else (1, 0, 2, 3)
            eb = par2 * 1536

            def piece(kb, c):
                if kb == 0:
                    return psb[:, eb + c * 256:eb + c * 256 + 128], (psb.name, par2, c), (0, 128)
                if kb == 3:
                    return (psb[:, eb + c * 256 + 128:eb + c * 256 + 256],
                            (psb.name, par2, c), (128, 256))
                mb = eb + 512 + c * 512
                if kb == 1:
                    return psb[:, mb:mb + 256], (psb.name, par2, 2 + c), (0, 256)
                return psb[:, mb + 256:mb + 512], (psb.name, par2, 2 + c), (0, 256)

            pkeys = [(psb.name, par2, b) for b in range(4)]
            vkeys = [(vring.name, ((gw - 2 + kb) % 10) * 2 + nch) for kb in kbs for nch in range(2)]
            for c in range(2):
                def emO(e, c=c, hp=hp, gw=gw, kbs=kbs):
                    h = 2 * hp + c
                    for j, kb in enumerate(kbs):
                        sv = (gw - 2 + kb) % 10
                        rhs, _, (a, z) = piece(kb, c)
                        inst = e.matmul(pw[c][c * 64:(c + 1) * 64, a:z],
                                        vr3[:, sv, h * 64:(h + 1) * 64], rhs,
                                        start=(j == 0), stop=(kb == 3),
                                        tile_position=(0, c * 64))
                    return inst
                bc.op("t", emO, reads=pkeys + vkeys, writes=[pw[c]])

            def emL(e, kbs=kbs):
                for c in range(2):
                    for j, kb in enumerate(kbs):
                        rhs, _, (a, z) = piece(kb, c)
                        inst = e.matmul(prl[0][0:1, c * 256 + a:c * 256 + z],
                                        bc.g_ones1[:, 0:1], rhs,
                                        start=(j == 0), stop=(kb == 3))
                return inst
            bc.op("t", emL, reads=pkeys + [bc.g_ones1], writes=[prl[0].name])
            sl = par2 * 512
            bc.op("v", lambda e, sl=sl: e.reciprocal(
                lrf[0:1, sl:sl + 512], prl[0][0:1, 0:512]),
                reads=[prl[0].name], writes=[(lrf.name, par2)])

            def emB(e, sl=sl):
                e.matmul(prl[0][:, 0:256], bc.g_onesr[:, :],
                         lrf[0:1, sl:sl + 256], start=True, stop=True)
                return e.matmul(prl[0][:, 256:512], bc.g_onesr[:, :],
                                lrf[0:1, sl + 256:sl + 512], start=True, stop=True)
            bc.op("t", emB, reads=[(lrf.name, par2), bc.g_onesr], writes=[prl[0].name])
            for c in range(2):
                bc.op("s", lambda e, hp=hp, wl=wl, c=c: e.activation(
                    o3w[c * 64:(c + 1) * 64, hp, wl * 256:(wl + 1) * 256],
                    pw[c][c * 64:(c + 1) * 64, 0:256], AF.Copy),
                    reads=[pw[c]], writes=[(osb.name, hp, wl, c)])
            bc.op("v", lambda e, hp=hp, wl=wl: e.tensor_tensor(
                o3w[0:64, hp, wl * 256:(wl + 1) * 256], o3w[0:64, hp, wl * 256:(wl + 1) * 256],
                prl[0][0:64, 0:256], AL.mult),
                reads=[(osb.name, hp, wl, 0), prl[0].name],
                writes=[(osb.name, hp, wl, "n")], inc=False)
            bc.op("v", lambda e, hp=hp, wl=wl: e.tensor_tensor(
                o3w[64:128, hp, wl * 256:(wl + 1) * 256], o3w[64:128, hp, wl * 256:(wl + 1) * 256],
                prl[0][64:128, 256:512], AL.mult),
                reads=[(osb.name, hp, wl, 1), prl[0].name],
                writes=[(osb.name, hp, wl, "n")])

        def emit_wo(gt, wl):
            seq, ti = divmod(gt, NT)
            par = gt % 2
            stg = stage[par]
            st3 = stg.ap().rearrange("p (kb t) -> p kb t", kb=KB)
            stkeys = [stg, (stg.name, "z")] if ti == 0 else [stg]
            okeys = [(osb.name, hp, wl, "n") for hp in range(KB)]
            wc4 = wch.ap().rearrange("p (r k m) -> p r k m", r=4, k=KB)

            def ldc(mb):
                bc.op("g", lambda e, mb=mb: e.dma_start(
                    wc4[:, mb % 4, :, :], woC[:, :, mb * 128:(mb + 1) * 128]),
                    reads=["woC"], writes=[(wch.name, mb % 4)], dma=True)
            for mb in range(3):
                ldc(mb)
            for mb in range(KB):
                ps = pw[mb % 2]

                def emWo(e, mb=mb, ps=ps, wl=wl):
                    for kb in range(KB):
                        inst = e.matmul(ps[:, 0:256], wc4[:, mb % 4, kb, :],
                                        o3w[:, kb, wl * 256:(wl + 1) * 256],
                                        start=(kb == 0), stop=(kb == KB - 1))
                    return inst
                bc.op("t", emWo, reads=[(wch.name, mb % 4)] + okeys, writes=[ps])
                if mb + 3 < KB:
                    ldc(mb + 3)
                bc.op("v", lambda e, mb=mb, ps=ps, st3=st3, wl=wl: e.tensor_tensor(
                    st3[:, mb, 1 + wl * 256:1 + (wl + 1) * 256], ps[:, 0:256],
                    st3[:, mb, 1 + wl * 256:1 + (wl + 1) * 256], AL.add),
                    reads=[ps] + stkeys, writes=[(stg.name, "y", mb, wl)])

        def emit_store(gt):
            seq, ti = divmod(gt, NT)
            c0 = seq * T + ti * TT
            par = gt % 2
            stg = stage[par]
            st3 = stg.ap().rearrange("p (kb t) -> p kb t", kb=KB)
            ykeys = [(stg.name, "y", mb, wl) for mb in range(KB) for wl in range(2)]
            bc.op("s", lambda e, st3=st3, c0=c0: e.dma_start(
                sview(dram[dst].ap(), c0, c0 + TT), st3[:, :, 1:TT + 1]),
                reads=ykeys, dma=True)

        pend = list(make_tile_ops(0))
        for op_ in pend:
            op_()
        for gt in range(NTILE):
            seq, ti = divmod(gt, NT)
            pend = list(make_tile_ops(gt + 1)) if gt + 1 < NTILE else []
            iters = [(wl, hp) for wl in range(2) for hp in range(KB)]
            npend = len(pend)
            taken = 0
            for idx, (wl, hp) in enumerate(iters):
                emit_win_S(gt, wl, hp)
                if idx >= 1:
                    emit_win_tail(gt, *iters[idx - 1])
                if idx == 8:
                    emit_wo(gt, 0)
                frac = cfg.get('ilfrac', 1.0)
                want = npend if cfg.get('noil') else min((idx + 1) * npend // 16, int(npend * frac))
                while taken < want:
                    pend[taken]()
                    taken += 1
            emit_win_tail(gt, *iters[15])
            emit_wo(gt, 1)
            emit_store(gt)
            if cfg.get("dbg"):
                seq_, ti_ = divmod(gt, NT)
                c0_ = seq_ * T + ti_ * TT
                par_ = gt % 2
                q3b_ = r3(qb[par_], KB)
                bc.op("g", lambda e, c0_=c0_, q3b_=q3b_: e.dma_start(
                    sview(dram["qkdbgT"].ap(), c0_, c0_ + TT)[:, 0:8, :], q3b_),
                    reads=[(qb[par_].name, m_) for m_ in range(8)], dma=True)
                for wl_ in range(2):
                    wg_ = ti_ * 2 + wl_
                    bc.op("g", lambda e, c0_=c0_, wl_=wl_, wg_=wg_: e.dma_start(
                        sview(dram["qkdbgT"].ap(), c0_ + wl_ * 256, c0_ + (wl_ + 1) * 256)[:, 8:16, :],
                        kr4[:, :, wg_ % 3, :]),
                        reads=[(kring.name, wg_ % 3)], dma=True)
                bc.op("g", lambda e, c0_=c0_: e.dma_start(
                    sview(dram["odbgT"].ap(), c0_, c0_ + TT), o3w),
                    reads=[(osb.name, hp_, wl_, k_) for hp_ in range(KB) for wl_ in range(2)
                           for k_ in ("n", 0, 1)],
                    dma=True)
                for ts_ in range(4):
                    sv_ = (gt * 4 + ts_) % 10
                    bc.op("g", lambda e, c0_=c0_, ts_=ts_, sv_=sv_: e.dma_start(
                        dram["vdbgT"].ap()[c0_ + ts_ * 128:c0_ + (ts_ + 1) * 128, :], vr3[:, sv_, :]),
                        reads=[(vring.name, sv_ * 2), (vring.name, sv_ * 2 + 1)], dma=True)
            while taken < npend:
                pend[taken]()
                taken += 1

def sec_ffn(bc, cfg, dram, l, src, dst):
    nc = bc.nc
    T, NSEQ = cfg["T"], cfg["NSEQ"]
    NT = T // TT
    NCB = 22
    with bc.section() as st:
        w1 = st.enter_context(nc.sbuf_tensor(f"ff{l}_w1", [128, KB * 5632], BF16))
        w2 = st.enter_context(nc.sbuf_tensor(f"ff{l}_w2", [128, NCB * 1024], BF16))
        stage = st.enter_context(nc.sbuf_tensor(f"ff{l}_st", [128, KB * (TT + 1)], F32))
        xnbf = st.enter_context(nc.sbuf_tensor(f"ff{l}_xn", [128, KB * TT], BF16))
        gc = st.enter_context(nc.sbuf_tensor(f"ff{l}_gc", [128, NCB * TT], BF16))
        atmp = st.enter_context(nc.sbuf_tensor(f"ff{l}_a", [128, TT], BF16))
        gel = st.enter_context(nc.sbuf_tensor(f"ff{l}_g", [128, TT], BF16))
        btot = st.enter_context(nc.sbuf_tensor(f"ff{l}_bt", [128, 2 * NCB], F32))
        b1sb = st.enter_context(nc.sbuf_tensor(f"ff{l}_b1", [128, 2 * NCB], F32))
        lnbb = st.enter_context(nc.sbuf_tensor(f"ff{l}_lb", [128, KB], BF16))
        rows = st.enter_context(nc.sbuf_tensor(f"ff{l}_rw", [1, 4 * TT], F32))
        pa = st.enter_context(nc.psum_tensor(f"ff{l}_pa", [128, 512], F32))
        pg = st.enter_context(nc.psum_tensor(f"ff{l}_pg", [128, 512], F32))
        pw = [st.enter_context(nc.psum_tensor(f"ff{l}_pw{i}", [128, 512], F32)) for i in range(2)]
        pln = st.enter_context(nc.psum_tensor(f"ff{l}_pln", [128, 512], F32))
        pl2 = st.enter_context(nc.psum_tensor(f"ff{l}_pl2", [128, 512], F32))
        pbc0 = st.enter_context(nc.psum_tensor(f"ff{l}_pb0", [128, 512], F32))
        pbc1 = st.enter_context(nc.psum_tensor(f"ff{l}_pb1", [128, 512], F32))

        w13 = r3(w1, KB)
        w23 = r3(w2, NCB)
        bt3 = r3(btot, 2)
        b13 = r3(b1sb, 2)
        xn3 = r3(xnbf, KB)
        gc3 = r3(gc, NCB)
        ones1 = bc.g_ones1
        onesr = bc.g_onesr

        bc.op("g", lambda e: e.dma_start(w13, wview(dram["iW1p"].ap()[l])), writes=[w1], dma=True)
        bc.op("g", lambda e: e.dma_start(w23, wview(dram["iW2p"].ap()[l])), writes=[w2], dma=True)
        bc.op("y", lambda e: e.dma_start(
            b13, dram["ib1p"].ap()[l]), writes=[b1sb], dma=True)
        bc.op("v", lambda e: e.tensor_copy(
            lnbb[:, :], bc.g_lnb.ap().rearrange("p (a c) -> p a c", a=6)[:, l, :]),
            reads=[bc.g_lnb], writes=[lnbb])

        def emvb(e):
            for half in range(2):
                for cb in range(NCB):
                    for kb in range(KB):
                        inst = e.matmul(pa[:, half * NCB + cb:half * NCB + cb + 1],
                                        w13[:, kb, (half * NCB + cb) * 128:(half * NCB + cb + 1) * 128],
                                        lnbb[:, kb:kb + 1], start=(kb == 0), stop=(kb == KB - 1))
            return inst
        bc.op("t", emvb, reads=[w1, lnbb], writes=[pa])
        bc.op("v", lambda e: e.tensor_tensor(btot[:, :], pa[:, 0:2 * NCB], b1sb[:, :], AL.add),
              reads=[pa, b1sb], writes=[btot])
        for kb in range(KB):
            bc.op("v", lambda e, kb=kb: e.tensor_scalar_mul(
                w13[:, kb, :], w13[:, kb, :],
                bc.g_lng.ap().rearrange("p (a c) -> p a c", a=6)[:, l, kb:kb + 1]),
                reads=[w1, bc.g_lng], writes=[w1])

        st3 = r3(stage, KB)
        for seq in range(NSEQ):
            for ti in range(NT):
                c0 = seq * T + ti * TT
                if ti == 0:
                    bc.op("y", lambda e, c0=c0: e.dma_start(
                        st3[:, :, 1:TT + 1], sview(dram[src].ap(), c0, c0 + TT)), writes=[stage], dma=True)
                    bc.op("g", lambda e: e.memset(st3[:, :, 0:1], 0.0), writes=[(stage.name, "z")])
                else:
                    bc.op("y", lambda e, c0=c0: e.dma_start(
                        st3, sview(dram[src].ap(), c0 - 1, c0 + TT)), writes=[stage], dma=True)
                stkeys = [stage, (stage.name, "z")] if ti == 0 else [stage]
                bc.op("v", lambda e: e.tensor_copy(xn3[:, 0:4, :], st3[:, 0:4, 1:TT + 1]),
                      reads=stkeys, writes=[(xnbf.name, 0)])
                bc.op("v", lambda e: e.tensor_copy(xn3[:, 4:8, :], st3[:, 4:8, 0:TT]),
                      reads=stkeys, writes=[(xnbf.name, 1)])
                xkeys = [(xnbf.name, 0), (xnbf.name, 1)]

                def s1(e):
                    for kb in range(KB):
                        inst = e.matmul(pln[0:1, :], ones1[:, :], xn3[:, kb, :],
                                        start=(kb == 0), stop=(kb == KB - 1))
                    return inst
                bc.op("t", s1, reads=xkeys + [ones1], writes=[pln])
                bc.op("v", lambda e: e.tensor_tensor(xn3, xn3, xn3, AL.mult), reads=xkeys, writes=[xnbf])

                def s2(e):
                    for kb in range(KB):
                        inst = e.matmul(pl2[0:1, :], ones1[:, :], xn3[:, kb, :],
                                        start=(kb == 0), stop=(kb == KB - 1))
                    return inst
                bc.op("t", s2, reads=[xnbf, ones1], writes=[pl2])

                def rsl(k):
                    return rows[0:1, k * TT:(k + 1) * TT]
                bc.op("s", lambda e: e.activation(rsl(0), pln[0:1, :], AF.Copy, scale=1.0 / D),
                      reads=[pln], writes=[(rows.name, "mu")])
                bc.op("s", lambda e: e.activation(rsl(1), pl2[0:1, :], AF.Copy, scale=1.0 / D),
                      reads=[pl2], writes=[(rows.name, "sq")])
                bc.op("v", lambda e: e.tensor_tensor(rsl(2), rsl(0), rsl(0), AL.mult),
                      reads=[(rows.name, "mu")], writes=[(rows.name, "mu2")])
                bc.op("v", lambda e: e.tensor_tensor(rsl(1), rsl(1), rsl(2), AL.subtract),
                      reads=[(rows.name, "sq"), (rows.name, "mu2")], writes=[(rows.name, "var")])
                bc.op("s", lambda e: e.activation(rsl(1), rsl(1), AF.Sqrt, bias=EPS),
                      reads=[(rows.name, "var"), bc.g_fc], writes=[(rows.name, "sd")])
                bc.op("v", lambda e: e.reciprocal(rsl(1), rsl(1)),
                      reads=[(rows.name, "sd")], writes=[(rows.name, "rstd")])
                bc.op("v", lambda e: e.tensor_tensor(rsl(2), rsl(0), rsl(1), AL.mult),
                      reads=[(rows.name, "mu"), (rows.name, "rstd")], writes=[(rows.name, "m2")])
                bc.op("t", lambda e: e.matmul(pbc0[:, :], onesr[:, :], rsl(2), start=True, stop=True),
                      reads=[(rows.name, "m2"), onesr], writes=[pbc0])
                bc.op("t", lambda e: e.matmul(pbc1[:, :], onesr[:, :], rsl(1), start=True, stop=True),
                      reads=[(rows.name, "rstd"), onesr], writes=[pbc1])
                rb1 = pbc1[:, None, :].broadcast_to([128, 4, TT])
                bc.op("v", lambda e, rb1=rb1: e.tensor_tensor(xn3[:, 0:4, :], st3[:, 0:4, 1:TT + 1], rb1, AL.mult),
                      reads=stkeys + [pbc1], writes=[(xnbf.name, 0)])
                bc.op("v", lambda e, rb1=rb1: e.tensor_tensor(xn3[:, 4:8, :], st3[:, 4:8, 0:TT], rb1, AL.mult),
                      reads=stkeys + [pbc1], writes=[(xnbf.name, 1)])
                rb0 = pbc0[:, None, :].broadcast_to([128, KB, TT])
                bc.op("v", lambda e, rb0=rb0: e.tensor_tensor(xn3, xn3, rb0, AL.subtract),
                      reads=xkeys + [pbc0], writes=[(xnbf.name, 0), (xnbf.name, 1)])
                for cb in range(NCB):
                    def ema(e, cb=cb):
                        for kb in range(KB):
                            inst = e.matmul(pa[:, :], w13[:, kb, cb * 128:(cb + 1) * 128], xn3[:, kb, :],
                                            start=(kb == 0), stop=(kb == KB - 1))
                        return inst
                    bc.op("t", ema, reads=[w1] + xkeys, writes=[pa])

                    def emg(e, cb=cb):
                        for kb in range(KB):
                            inst = e.matmul(pg[:, :], w13[:, kb, (NCB + cb) * 128:(NCB + cb + 1) * 128],
                                            xn3[:, kb, :], start=(kb == 0), stop=(kb == KB - 1))
                        return inst
                    bc.op("t", emg, reads=[w1] + xkeys, writes=[pg])
                    bc.op("s", lambda e, cb=cb: e.activation(atmp[:, :], pa[:, :], AF.Identity,
                                                             bias=bt3[:, 0, cb:cb + 1]),
                          reads=[pa, btot], writes=[atmp])
                    bc.op("s", lambda e, cb=cb: e.activation(gel[:, :], pg[:, :], AF.Gelu,
                                                             bias=bt3[:, 1, cb:cb + 1]),
                          reads=[pg, btot], writes=[gel])
                    bc.op("v", lambda e, cb=cb: e.tensor_tensor(gc3[:, cb, :], atmp[:, :], gel[:, :], AL.mult),
                          reads=[atmp, gel], writes=[(gc.name, cb)])
                gkeys = [(gc.name, cb) for cb in range(NCB)]
                for mb in range(KB):
                    ps = pw[mb % 2]

                    def emw2(e, mb=mb, ps=ps):
                        for cb in range(NCB):
                            inst = e.matmul(ps[:, :], w23[:, cb, mb * 128:(mb + 1) * 128], gc3[:, cb, :],
                                            start=(cb == 0), stop=(cb == NCB - 1))
                        return inst
                    bc.op("t", emw2, reads=[w2] + gkeys, writes=[ps])
                    bc.op("v", lambda e, mb=mb, ps=ps: e.scalar_tensor_tensor(
                        st3[:, mb, 1:TT + 1], ps[:, :],
                        bc.g_b2.ap().rearrange("p (a c) -> p a c", a=6)[:, l, mb:mb + 1],
                        st3[:, mb, 1:TT + 1], AL.add, AL.add),
                        reads=[ps, bc.g_b2] + stkeys, writes=[(stage.name, "y", mb)])
                ykeys = [(stage.name, "y", mb) for mb in range(KB)]
                bc.op("s", lambda e, c0=c0: e.dma_start(
                    sview(dram[dst].ap(), c0, c0 + TT), st3[:, :, 1:TT + 1]),
                    reads=ykeys, dma=True)




def kernel(**inputs):
    x = np.asarray(inputs["L0_attn_x"])
    B, T, _ = x.shape
    n_cores = 8
    cfg = {"T": T, "NSEQ": B // n_cores, "nlayer": 6}
    maps = host_layout(inputs, n_cores, T, cfg["nlayer"])
    nc = build_program(cfg)
    res = run_bass_kernel_spmd(nc, maps, core_ids=list(range(n_cores)))
    outs = []
    bpc = B // n_cores
    for r in res.results:
        o = r["outT"].reshape(D, bpc, T).transpose(1, 2, 0)
        outs.append(o)
    return np.ascontiguousarray(np.concatenate(outs, axis=0), dtype=np.float32)


# version 16 (final)
# speedup vs baseline: 1.0999x; 1.0999x over previous
import contextlib

import numpy as np
import ml_dtypes

import concourse.bass as bass
import concourse.mybir as mybir
from concourse.bass_utils import run_bass_kernel_spmd

F32 = mybir.dt.float32
BF16 = mybir.dt.bfloat16
AF = mybir.ActivationFunctionType
AL = mybir.AluOpType
AX = mybir.AxisListType
bfnp = ml_dtypes.bfloat16

D = 1024
KB = 8
TT = 512
CH = 256
NHR = 4
DKR = 256
DVR = 512
NHI = 16
DHI = 64
NEG = -1.0e30
EPS = 1e-5

ENGS = ("t", "v", "s", "g", "y")
DMAQ = {"y": "yD", "s": "sD", "g": "gD"}
NDMASEM = 4
DMA_SEMS = {f"{q}{i}" for q in DMAQ.values() for i in range(NDMASEM)}


class BC:

    def __init__(self, nc):
        self.nc = nc
        self.es = contextlib.ExitStack()
        self.sems = {}
        for name in list(ENGS) + sorted(DMA_SEMS):
            self.sems[name] = self.es.enter_context(nc.semaphore("sem_" + name))
        self.cnt = {name: 0 for name in self.sems}
        self.qrot = {q: 0 for q in DMAQ.values()}
        self.queues = {e: [] for e in ENGS}
        self.hi = {}
        self.w = {}
        self.r = {}

    @staticmethod
    def _nk(x):
        if isinstance(x, bass.AP):
            return (x.tensor.name, None)
        if hasattr(x, "name"):
            return (x.name, None)
        if isinstance(x, tuple):
            return (x[0] if not hasattr(x[0], "name") else x[0].name,
                    x[1:] if len(x) > 2 else x[1])
        return (x, None)

    def _wdeps(self, name, slot):
        d = self.w.get(name)
        if not d:
            return []
        if slot is None:
            return list(d.values())
        out = []
        if None in d:
            out.append(d[None])
        if slot in d:
            out.append(d[slot])
        return out

    def _rdeps(self, name, slot):
        d = self.r.get(name)
        if not d:
            return []
        if slot is None:
            return [p for lst in d.values() for p in lst]
        return list(d.get(None, ())) + list(d.get(slot, ()))

    def op(self, eng, emit, reads=(), writes=(), dma=False, inc=True):
        need = {}

        def add(dep):
            if dep is None:
                return
            sem, v = dep
            if v > need.get(sem, 0):
                need[sem] = v
        rk = [self._nk(x) for x in reads]
        wk = [self._nk(x) for x in writes]
        for name, slot in rk:
            for dep in self._wdeps(name, slot):
                add(dep)
        for name, slot in wk:
            for dep in self._wdeps(name, slot):
                add(dep)
            for dep in self._rdeps(name, slot):
                add(dep)
        waits = []
        for sem, v in need.items():
            if self.hi.get((eng, sem), 0) < v:
                waits.append((sem, v))
                self.hi[(eng, sem)] = v
        if dma:
            q = DMAQ[eng]
            sem_name = f"{q}{self.qrot[q] % NDMASEM}"
            self.qrot[q] += 1
            prev = self.cnt[sem_name]
            if prev and self.hi.get((eng, sem_name), 0) < prev:
                waits.append((sem_name, prev))
                self.hi[(eng, sem_name)] = prev
        else:
            sem_name = eng
        prod = None
        if inc:
            self.cnt[sem_name] += 1
            prod = (sem_name, self.cnt[sem_name])
        self.queues[eng].append((waits, emit, sem_name if inc else None))
        for name, slot in wk:
            d = self.w.setdefault(name, {})
            if slot is None:
                d.clear()
            d[slot] = prod
            rr = self.r.get(name)
            if rr:
                if slot is None:
                    rr.clear()
                else:
                    rr.pop(slot, None)
        if prod:
            for name, slot in rk:
                self.r.setdefault(name, {}).setdefault(slot, []).append(prod)
        return prod

    def barrier(self):
        snap = dict(self.cnt)
        for eng in ENGS:
            waits = []
            for sem, v in snap.items():
                if v and self.hi.get((eng, sem), 0) < v:
                    waits.append((sem, v))
                    self.hi[(eng, sem)] = v
            if waits:
                self.queues[eng].append((waits, None, None))
        self.w.clear()
        self.r.clear()

    def flush(self):
        sems = self.sems

        def run(e, q):
            for waits, emit, sem_name in q:
                for s, v in waits:
                    step = 16 if s in DMA_SEMS else 1
                    e.wait_ge(sems[s], v * step)
                if emit is not None:
                    inst = emit(e)
                    if sem_name is not None:
                        inst.then_inc(sems[sem_name], 16 if sem_name in DMA_SEMS else 1)
        qs = self.queues
        with self.nc.Block() as block:
            block.tensor(lambda e: run(e, qs["t"]))
            block.vector(lambda e: run(e, qs["v"]))
            block.scalar(lambda e: run(e, qs["s"]))
            block.gpsimd(lambda e: run(e, qs["g"]))
            block.sync(lambda e: run(e, qs["y"]))
        self.queues = {e: [] for e in ENGS}

    @contextlib.contextmanager
    def section(self):
        with contextlib.ExitStack() as st:
            yield st
            self.barrier()



def host_consts(T):
    c = {}
    c["ident"] = np.eye(128, dtype=np.float32).astype(bfnp)
    c["ident32"] = np.eye(128, dtype=np.float32)
    t = np.arange(T, dtype=np.float64)
    invf = 1.0 / (10000.0 ** (np.arange(0, DKR, 2) / DKR))
    fr = invf[:, None] * t[None, :]
    c["rcos"] = np.cos(fr).astype(np.float32)
    c["rsin"] = np.sin(fr).astype(np.float32)
    gam = 1.0 - 2.0 ** (-5.0 - np.arange(NHR))
    i = np.arange(CH)
    diff = i[None, :] - i[:, None]
    mT = np.where(diff >= 0, gam[:, None, None] ** diff, 0.0) / 16.0
    c["maskT"] = np.ascontiguousarray(
        mT.reshape(NHR, 2, 128, CH).transpose(2, 0, 1, 3)).astype(bfnp)
    cd = (gam[:, None] ** (i + 1.0)) / 16.0
    c["cdq"] = np.ascontiguousarray(np.broadcast_to(cd[None], (128, NHR, CH))).astype(bfnp)
    kd = gam[:, None] ** (CH - 1.0 - i)
    c["kdec"] = np.ascontiguousarray(kd.reshape(NHR, 2, 128).transpose(2, 0, 1)).astype(np.float32)
    half = np.arange(0, DHI, 2, dtype=np.float64)
    invf2 = 1.0 / (10000.0 ** (half / DHI))
    fr2 = t[:, None] * invf2[None, :]
    base = (half + 0.4 * DHI) / (1.4 * DHI)
    power = (t - T // 2) / (256 // 2)
    scale = base[None, :] ** power[:, None]
    cos = np.cos(fr2)
    sin = np.sin(fr2)
    j = np.arange(DHI)
    sgn = np.where(j < 32, -1.0, 1.0)
    cosd = cos[:, j % 32]
    sind = sin[:, j % 32]
    scd = scale[:, j % 32]
    qa = (cosd * scd).T
    qb = sgn[:, None] * sind.T * scd.T
    ka = (cosd / scd).T
    kb_ = sgn[:, None] * sind.T / scd.T
    for name, tab in (("iqa", qa), ("iqb", qb), ("ika", ka), ("ikb", kb_)):
        c[name] = np.ascontiguousarray(np.vstack([tab, tab])).astype(np.float32)
    r = np.arange(128)[:, None]
    cc = np.arange(512)[None, :]
    m0 = np.where((cc >= r) & (cc <= r + 256), 0.0, NEG)
    m1 = np.where((cc >= r + 128) & (cc <= r + 384), 0.0, NEG)
    m0w = m0.copy(); m0w[:, :256] = NEG
    m1w = m1.copy(); m1w[:, :256] = NEG
    c["imask"] = np.ascontiguousarray(np.stack([m0, m1, m0w, m1w], axis=1)).astype(bfnp)
    z = np.zeros((128, 33), dtype=np.float32)
    z[np.arange(128), 16 + np.arange(128) // 64] = 1.0
    c["zl2"] = z.astype(bfnp)
    bcl = np.zeros((16, KB * 128), dtype=np.float32)
    for b in range(KB):
        for m in range(128):
            bcl[2 * b + m // 64, b * 128 + m] = 1.0
    c["bclhs"] = bcl.astype(bfnp)
    z7 = np.zeros((128, 7), dtype=np.float32)
    z7[:, 3] = 1.0
    c["z7"] = z7.astype(bfnp)
    z7t4 = np.zeros((4, 4 * 128), dtype=np.float32)
    for h in range(4):
        z7t4[h, h * 128:(h + 1) * 128] = 1.0
    c["z7t"] = z7t4
    c["fconsts"] = np.full((128, 1), EPS, dtype=np.float32)
    c["onescol"] = np.ones((128, 1), dtype=np.float32).astype(bfnp)
    c["onesrow32"] = np.ones((1, 128), dtype=np.float32)
    return c


def host_layout(ins, n_cores, T, nlayer):
    x = np.asarray(ins["L0_attn_x"], dtype=np.float32)
    B = x.shape[0]
    bpc = B // n_cores
    L = 6
    sh = {}
    sh["rWqT"] = np.ascontiguousarray(np.asarray(ins["L0_attn_Wq"], dtype=np.float32).T)
    sh["rWkT"] = np.ascontiguousarray(np.asarray(ins["L0_attn_Wk"], dtype=np.float32).T)
    sh["rWvT"] = np.ascontiguousarray(np.asarray(ins["L0_attn_Wv"], dtype=np.float32).T)
    sh["rWgT"] = np.ascontiguousarray(np.asarray(ins["L0_attn_Wg"], dtype=np.float32).T)
    sh["rWoT"] = np.ascontiguousarray(np.asarray(ins["L0_attn_Wo"], dtype=np.float32).T)
    sh["gnw4"] = np.ascontiguousarray(np.asarray(ins["L0_attn_gnorm_w"], dtype=np.float32).reshape(4, 128).T)
    sh["iWqkv"] = np.asarray(ins["L0_ffn_Wqkv"], dtype=np.float32)
    sh["iWo"] = np.asarray(ins["L0_ffn_Wo"], dtype=np.float32)
    W1 = np.asarray(ins["L0_ffn_W1"], dtype=np.float32)
    W1p = np.zeros((L, D, 2, 2816), dtype=np.float32)
    W1p[:, :, 0, :2730] = W1[:, :, :2730]
    W1p[:, :, 1, :2730] = W1[:, :, 2730:]
    sh["iW1p"] = np.ascontiguousarray(W1p.reshape(L, D, 5632))
    b1 = np.asarray(ins["L0_ffn_b1"], dtype=np.float32)
    b1p = np.zeros((L, 2, 2816), dtype=np.float32)
    b1p[:, 0, :2730] = b1[:, :2730]
    b1p[:, 1, :2730] = b1[:, 2730:]
    sh["ib1p"] = np.ascontiguousarray(b1p.reshape(L, 2, 22, 128).transpose(0, 3, 1, 2))
    W2 = np.asarray(ins["L0_ffn_W2"], dtype=np.float32)
    W2p = np.zeros((L, 2816, D), dtype=np.float32)
    W2p[:, :2730] = W2
    sh["iW2p"] = W2p
    sh["ib2c"] = np.ascontiguousarray(np.asarray(ins["L0_ffn_b2"], dtype=np.float32).reshape(L, 8, 128).transpose(0, 2, 1))
    sh["ilng"] = np.ascontiguousarray(np.asarray(ins["L0_ffn_ln_g"], dtype=np.float32).reshape(L, 8, 128).transpose(0, 2, 1))
    sh["ilnb"] = np.ascontiguousarray(np.asarray(ins["L0_ffn_ln_b"], dtype=np.float32).reshape(L, 8, 128).transpose(0, 2, 1))
    qs = np.asarray(ins["L0_ffn_q_scale"], dtype=np.float32)
    ks = np.asarray(ins["L0_ffn_k_scale"], dtype=np.float32)
    qsd = np.concatenate([qs, qs], axis=1)
    ksd = np.concatenate([ks, ks], axis=1)
    swap = (np.arange(128) ^ 32)
    sh["iqs"] = np.ascontiguousarray(qsd.T)
    sh["iks"] = np.ascontiguousarray(ksd.T)
    sh["iqssw"] = np.ascontiguousarray(qsd[:, swap].T)
    sh["ikssw"] = np.ascontiguousarray(ksd[:, swap].T)
    sh.update(host_consts(T))
    maps = []
    for cid in range(n_cores):
        xc = x[cid * bpc:(cid + 1) * bpc]
        m = dict(sh)
        m["xT"] = np.ascontiguousarray(xc.transpose(2, 0, 1).reshape(D, bpc * T))
        maps.append(m)
    return maps



def sview(ap, c0, c1):
    return ap.rearrange("(kb p) t -> p kb t", p=128)[:, :, c0:c1]


def wview(ap):
    return ap.rearrange("(kb p) m -> p kb m", p=128)


def r3(t, n):
    return t.ap().rearrange("p (a b) -> p a b", a=n)



def emit_ln(bc, lnb, ysrc3, rdkeys, out3, outw):
    ybf, rows = lnb["ybf"], lnb["rows"]
    pln, pl2, pbc0, pbc1 = lnb["pln"], lnb["pl2"], lnb["pbc0"], lnb["pbc1"]
    ones1, onesr = lnb["ones1"], lnb["onesr"]
    yb3 = r3(ybf, KB)
    bc.op("v", lambda e: e.tensor_copy(yb3, ysrc3), reads=rdkeys, writes=[ybf])

    def s1(e):
        for kb in range(KB):
            inst = e.matmul(pln[0:1, :], ones1[:, :], yb3[:, kb, :], start=(kb == 0), stop=(kb == KB - 1))
        return inst
    bc.op("t", s1, reads=[ybf, ones1], writes=[pln])
    bc.op("v", lambda e: e.tensor_tensor(yb3, yb3, yb3, AL.mult), reads=[ybf], writes=[ybf])

    def s2(e):
        for kb in range(KB):
            inst = e.matmul(pl2[0:1, :], ones1[:, :], yb3[:, kb, :], start=(kb == 0), stop=(kb == KB - 1))
        return inst
    bc.op("t", s2, reads=[ybf, ones1], writes=[pl2])
    MU, VA, M2 = 0, 1, 2
    def rsl(k):
        return rows[0:1, k * TT:(k + 1) * TT]
    bc.op("s", lambda e: e.activation(rsl(MU), pln[0:1, :], AF.Copy, scale=1.0 / D),
          reads=[pln], writes=[(rows.name, "mu")])
    bc.op("s", lambda e: e.activation(rsl(VA), pl2[0:1, :], AF.Copy, scale=1.0 / D),
          reads=[pl2], writes=[(rows.name, "sq")])
    bc.op("v", lambda e: e.tensor_tensor(rsl(M2), rsl(MU), rsl(MU), AL.mult),
          reads=[(rows.name, "mu")], writes=[(rows.name, "mu2")])
    bc.op("v", lambda e: e.tensor_tensor(rsl(VA), rsl(VA), rsl(M2), AL.subtract),
          reads=[(rows.name, "sq"), (rows.name, "mu2")], writes=[(rows.name, "var")])
    bc.op("s", lambda e: e.activation(rsl(VA), rsl(VA), AF.Sqrt, bias=EPS),
          reads=[(rows.name, "var"), bc.g_fc], writes=[(rows.name, "sd")])
    bc.op("v", lambda e: e.reciprocal(rsl(VA), rsl(VA)),
          reads=[(rows.name, "sd")], writes=[(rows.name, "rstd")])
    bc.op("t", lambda e: e.matmul(pbc0[:, :], onesr[:, :], rsl(MU), start=True, stop=True),
          reads=[(rows.name, "mu"), onesr], writes=[pbc0])
    bc.op("t", lambda e: e.matmul(pbc1[:, :], onesr[:, :], rsl(VA), start=True, stop=True),
          reads=[(rows.name, "rstd"), onesr], writes=[pbc1])
    mub = pbc0[:, None, :].broadcast_to([128, KB, TT])
    rsb = pbc1[:, None, :].broadcast_to([128, KB, TT])
    bc.op("v", lambda e: e.scalar_tensor_tensor(out3, ysrc3, 1.0, mub, AL.mult, AL.subtract),
          reads=rdkeys + [pbc0], writes=[outw])
    bc.op("v", lambda e: e.tensor_tensor(out3, out3, rsb, AL.mult),
          reads=[outw, pbc1], writes=[outw])


def mk_lnbufs(bc, st, pfx, nc):
    b = {}
    b["ybf"] = st.enter_context(nc.sbuf_tensor(pfx + "_lnyb", [128, KB * TT], BF16))
    b["rows"] = st.enter_context(nc.sbuf_tensor(pfx + "_lnrw", [1, 4 * TT], F32))
    b["pln"] = st.enter_context(nc.psum_tensor(pfx + "_pln", [128, 512], F32))
    b["pl2"] = st.enter_context(nc.psum_tensor(pfx + "_pl2", [128, 512], F32))
    b["pbc0"] = st.enter_context(nc.psum_tensor(pfx + "_pb0", [128, 512], F32))
    b["pbc1"] = st.enter_context(nc.psum_tensor(pfx + "_pb1", [128, 512], F32))
    b["ones1"] = bc.g_ones1
    b["onesr"] = bc.g_onesr
    return b



def sec_retnet_A(bc, cfg, dram):
    nc = bc.nc
    T, NSEQ = cfg["T"], cfg["NSEQ"]
    NT = T // TT
    with bc.section() as st:
        wq = st.enter_context(nc.sbuf_tensor("ra_wq", [128, KB * 1024], BF16))
        wk = st.enter_context(nc.sbuf_tensor("ra_wk", [128, KB * 1024], BF16))
        wv = st.enter_context(nc.sbuf_tensor("ra_wv", [128, KB * 2048], BF16))
        xbf = [st.enter_context(nc.sbuf_tensor(f"ra_x{i}", [128, KB * TT], BF16)) for i in range(2)]
        qsb = st.enter_context(nc.sbuf_tensor("ra_q", [128, KB * TT], BF16))
        ksb = st.enter_context(nc.sbuf_tensor("ra_k", [128, KB * TT], BF16))
        qrt = st.enter_context(nc.sbuf_tensor("ra_qr", [128, KB * TT], BF16))
        krt = st.enter_context(nc.sbuf_tensor("ra_kr", [128, KB * TT], BF16))
        qcd = st.enter_context(nc.sbuf_tensor("ra_qc", [128, KB * CH], BF16))
        tA = st.enter_context(nc.sbuf_tensor("ra_tA", [128, 4 * TT], BF16))
        tB = st.enter_context(nc.sbuf_tensor("ra_tB", [128, 4 * TT], BF16))
        vsb = st.enter_context(nc.sbuf_tensor("ra_v", [128, 4 * 2048], BF16))
        ktk = st.enter_context(nc.sbuf_tensor("ra_kt", [128, 2 * DKR], BF16))
        ptb = st.enter_context(nc.sbuf_tensor("ra_pt", [128, 2 * CH], BF16))
        ssb = st.enter_context(nc.sbuf_tensor("ra_S", [128, NHR * 2 * DVR], F32))
        ssbb = st.enter_context(nc.sbuf_tensor("ra_Sb", [128, NHR * 2 * DVR], BF16))
        orw = st.enter_context(nc.sbuf_tensor("ra_or", [128, 16 * TT], BF16))
        rct = [st.enter_context(nc.sbuf_tensor("ra_rc0", [128, TT], F32))]
        rst_ = [st.enter_context(nc.sbuf_tensor("ra_rs0", [128, TT], F32))]
        cdq = st.enter_context(nc.sbuf_tensor("ra_cd", [128, NHR * CH], BF16))
        mskT = st.enter_context(nc.sbuf_tensor("ra_mT", [128, NHR * 2 * CH], BF16))
        kdec = st.enter_context(nc.sbuf_tensor("ra_kd", [128, NHR * 2], F32))
        pj = [st.enter_context(nc.psum_tensor(f"ra_pj{i}", [128, 512], F32)) for i in range(2)]
        ppt = st.enter_context(nc.psum_tensor("ra_ppt", [128, 512], F32))
        pkt = st.enter_context(nc.psum_tensor("ra_pkt", [128, 1024], BF16))
        po = [st.enter_context(nc.psum_tensor(f"ra_po{i}", [128, 512], F32)) for i in range(2)]
        pd = [st.enter_context(nc.psum_tensor(f"ra_pd{i}", [128, 512], F32)) for i in range(2)]

        ident = bc.g_ident
        bc.op("g", lambda e: e.dma_start(r3(wq, KB), wview(dram["rWqT"].ap())), writes=[wq], dma=True)
        bc.op("g", lambda e: e.dma_start(r3(wk, KB), wview(dram["rWkT"].ap())), writes=[wk], dma=True)
        bc.op("g", lambda e: e.dma_start(r3(wv, KB), wview(dram["rWvT"].ap())), writes=[wv], dma=True)
        bc.op("y", lambda e: e.dma_start(
            cdq.ap().rearrange("p (h t) -> p h t", h=NHR), dram["cdq"].ap()), writes=[cdq], dma=True)
        bc.op("y", lambda e: e.dma_start(
            mskT.ap().rearrange("p (h j c) -> p h j c", h=NHR, j=2), dram["maskT"].ap()), writes=[mskT], dma=True)
        bc.op("y", lambda e: e.dma_start(
            kdec.ap().rearrange("p (h j) -> p h j", h=NHR), dram["kdec"].ap()), writes=[kdec], dma=True)

        q3, k3 = r3(qsb, KB), r3(ksb, KB)
        qr3, kr3, qc3 = r3(qrt, KB), r3(krt, KB), r3(qcd, KB)
        v3 = r3(vsb, 4)
        o3 = r3(orw, 16)
        S3 = ssb.ap().rearrange("p (h z n) -> p h z n", h=NHR, z=2)
        S3b = ssbb.ap().rearrange("p (h z n) -> p h z n", h=NHR, z=2)
        w3q, w3k, w3v = r3(wq, KB), r3(wk, KB), r3(wv, KB)
        cd3 = cdq.ap().rearrange("p (h t) -> p h t", h=NHR)
        mT3 = mskT.ap().rearrange("p (h j c) -> p h j c", h=NHR, j=2)
        kt3 = r3(ktk, 2)
        pt3 = r3(ptb, 2)
        pkt3 = r3(pkt, 2)
        ppt3 = r3(ppt, 2)

        qkeys = [(qrt.name, "ev"), (qrt.name, "od")]
        kkeys = [(krt.name, "ev"), (krt.name, "od")]

        for seq in range(NSEQ):
            bc.op("g", lambda e: e.memset(ssb[:, :], 0.0), writes=[ssb])
            bc.op("g", lambda e: e.memset(ssbb[:, :], 0.0), writes=[ssbb])
            for ti in range(NT):
                gt = seq * NT + ti
                c0 = seq * T + ti * TT
                xb = xbf[gt % 2]
                x3 = r3(xb, KB)
                bc.op("g", lambda e, x3=x3, c0=c0: e.dma_start(x3, sview(dram["xT"].ap(), c0, c0 + TT)),
                      writes=[xb], dma=True)
                rc, rs = rct[0], rst_[0]
                tc0 = ti * TT
                bc.op("y", lambda e, rc=rc, tc0=tc0: e.dma_start(rc[:, :], dram["rcos"].ap()[:, tc0:tc0 + TT]),
                      writes=[rc], dma=True)
                bc.op("y", lambda e, rs=rs, tc0=tc0: e.dma_start(rs[:, :], dram["rsin"].ap()[:, tc0:tc0 + TT]),
                      writes=[rs], dma=True)
                for wt, wbuf, dst, dname in ((w3q, wq, q3, qsb.name), (w3k, wk, k3, ksb.name)):
                    for mb in range(KB):
                        ps = pj[mb % 2]

                        def em(e, wt=wt, mb=mb, x3=x3, ps=ps):
                            for kb in range(KB):
                                inst = e.matmul(ps[:, :], wt[:, kb, mb * 128:(mb + 1) * 128], x3[:, kb, :],
                                                start=(kb == 0), stop=(kb == KB - 1))
                            return inst
                        bc.op("t", em, reads=[wbuf, xb], writes=[ps])
                        bc.op("s", lambda e, dst=dst, mb=mb, ps=ps: e.activation(dst[:, mb, :], ps[:, :], AF.Copy),
                              reads=[ps], writes=[(dname, mb)])
                for ts in range(4):
                    for nch in range(4):
                        ps = pj[(ts * 4 + nch) % 2]

                        def emv(e, ts=ts, nch=nch, x3=x3, ps=ps):
                            for kb in range(KB):
                                inst = e.matmul(ps[:, :], x3[:, kb, ts * 128:(ts + 1) * 128],
                                                w3v[:, kb, nch * 512:(nch + 1) * 512],
                                                start=(kb == 0), stop=(kb == KB - 1))
                            return inst
                        bc.op("t", emv, reads=[wv, xb], writes=[ps])
                        bc.op("s", lambda e, ts=ts, nch=nch, ps=ps: e.activation(
                            v3[:, ts, nch * 512:(nch + 1) * 512], ps[:, :], AF.Copy),
                            reads=[ps], writes=[(vsb.name, ts * 4 + nch)])
                for src, sname, dst, dkeys in ((q3, qsb.name, qr3, qkeys), (k3, ksb.name, kr3, kkeys)):
                    skeys = [(sname, mb) for mb in range(KB)]
                    a4 = r3(tA, 4)
                    b4 = r3(tB, 4)
                    ev = src[:, 0:8:2, :]
                    od = src[:, 1:8:2, :]
                    rcb = rc[:, None, :].broadcast_to([128, 4, TT])
                    rsb = rs[:, None, :].broadcast_to([128, 4, TT])
                    bc.op("v", lambda e, a4=a4, ev=ev, rcb=rcb: e.tensor_tensor(a4, ev, rcb, AL.mult),
                          reads=skeys + [rc], writes=[tA])
                    bc.op("v", lambda e, b4=b4, od=od, rsb=rsb: e.tensor_tensor(b4, od, rsb, AL.mult),
                          reads=skeys + [rs], writes=[tB])
                    bc.op("v", lambda e, dst=dst, a4=a4, b4=b4: e.tensor_tensor(dst[:, 0:8:2, :], a4, b4, AL.subtract),
                          reads=[tA, tB], writes=[dkeys[0]])
                    bc.op("v", lambda e, a4=a4, od=od, rcb=rcb: e.tensor_tensor(a4, od, rcb, AL.mult),
                          reads=skeys + [rc], writes=[tA])
                    bc.op("v", lambda e, b4=b4, ev=ev, rsb=rsb: e.tensor_tensor(b4, ev, rsb, AL.mult),
                          reads=skeys + [rs], writes=[tB])
                    bc.op("v", lambda e, dst=dst, a4=a4, b4=b4: e.tensor_tensor(dst[:, 1:8:2, :], a4, b4, AL.add),
                          reads=[tA, tB], writes=[dkeys[1]])
                qrh = qrt.ap().rearrange("p (h u t) -> p h u t", h=4, u=2)
                qch = qcd.ap().rearrange("p (h u t) -> p h u t", h=4, u=2)
                for ci in range(2):
                    tcol = ci * CH
                    bc.op("v", lambda e, qch=qch, qrh=qrh, tcol=tcol: e.tensor_tensor(
                        qch, qrh[:, :, :, tcol:tcol + CH],
                        cd3[:, :, None, :].broadcast_to([128, 4, 2, CH]), AL.mult),
                        reads=qkeys + [cdq], writes=[qcd])
                    for h in range(NHR):
                        kblk = (2 * h, 2 * h + 1)

                        def em_pt(e, h=h, tcol=tcol, kblk=kblk):
                            for js in range(2):
                                for z in range(2):
                                    inst = e.matmul(ppt3[:, js, :],
                                                    kr3[:, kblk[z], tcol + js * 128:tcol + (js + 1) * 128],
                                                    qr3[:, kblk[z], tcol:tcol + CH],
                                                    start=(js == 0 and z == 0), stop=(js == 1 and z == 1))
                            return inst
                        bc.op("t", em_pt, reads=qkeys + kkeys, writes=[ppt])
                        bc.op("v", lambda e, h=h: e.tensor_tensor(pt3, ppt3, mT3[:, h, :, :], AL.mult),
                              reads=[ppt, mskT], writes=[ptb])

                        def em_kt(e, h=h, tcol=tcol, kblk=kblk):
                            for js in range(2):
                                for z in range(2):
                                    inst = e.transpose(pkt3[:, js, z * 128:(z + 1) * 128],
                                                       kr3[:, kblk[z], tcol + js * 128:tcol + (js + 1) * 128],
                                                       ident[:, :])
                            return inst
                        bc.op("t", em_kt, reads=kkeys + [ident], writes=[pkt])
                        for js in range(2):
                            bc.op("s", lambda e, js=js, h=h: e.activation(
                                kt3[:, js, :], pkt3[:, js, 0:256], AF.Copy,
                                scale=kdec.ap().rearrange("p (h j) -> p h j", h=NHR)[:, h, js:js + 1]),
                                reads=[pkt, kdec], writes=[(ktk.name, js)])

                        def em_o(e, h=h, ci=ci, tcol=tcol, kblk=kblk):
                            for bank in range(2):
                                first = True
                                for s_ in range(2):
                                    dvs = bank * 2 + s_
                                    col = slice(s_ * CH, (s_ + 1) * CH)
                                    for js in range(2):
                                        e.matmul(po[bank][:, col],
                                                 v3[:, ci * 2 + js, h * 512 + dvs * 128:h * 512 + (dvs + 1) * 128],
                                                 pt3[:, js, :], start=first, stop=False)
                                        first = False
                                    for z in range(2):
                                        last = (s_ == 1 and z == 1)
                                        inst = e.matmul(po[bank][:, col],
                                                        S3b[:, h, z, dvs * 128:(dvs + 1) * 128],
                                                        qc3[:, kblk[z], :],
                                                        start=False, stop=last)
                            return inst
                        bc.op("t", em_o, reads=[vsb, ptb, ssbb, qcd], writes=[po[0], po[1]])
                        for bank in range(2):
                            bc.op("s", lambda e, bank=bank, h=h, tcol=tcol: e.activation(
                                o3[:, 4 * h + 2 * bank:4 * h + 2 * bank + 2, tcol:tcol + CH],
                                po[bank].ap().rearrange("p (s c) -> p s c", s=2), AF.Copy),
                                reads=[po[bank]], writes=[(orw.name, (h, bank, ci))])

                        def em_d(e, h=h, ci=ci):
                            for z in range(2):
                                for js in range(2):
                                    inst = e.matmul(pd[z][:, :], kt3[:, js, z * 128:(z + 1) * 128],
                                                    v3[:, ci * 2 + js, h * 512:h * 512 + 512],
                                                    start=(js == 0), stop=(js == 1))
                            return inst
                        bc.op("t", em_d, reads=[(ktk.name, 0), (ktk.name, 1), vsb], writes=[pd[0], pd[1]])
                        gC = float((1.0 - 2.0 ** (-5.0 - h)) ** CH)
                        for z in range(2):
                            bc.op("v", lambda e, z=z, h=h, gC=gC: e.scalar_tensor_tensor(
                                S3[:, h, z, :], S3[:, h, z, :], gC, pd[z][:, :], AL.mult, AL.add),
                                reads=[pd[z], ssb], writes=[ssb])
                            bc.op("v", lambda e, z=z, h=h: e.tensor_copy(S3b[:, h, z, :], S3[:, h, z, :]),
                                  reads=[ssb], writes=[ssbb])
                okeys = [(orw.name, (h, bank, ci)) for h in range(4) for bank in range(2) for ci in range(2)]
                bc.op("s", lambda e, o3=o3, c0=c0: e.dma_start(sview(dram["orawT"].ap(), c0, c0 + TT), o3),
                      reads=okeys, dma=True)


def sec_retnet_BC(bc, cfg, dram):
    nc = bc.nc
    T, NSEQ = cfg["T"], cfg["NSEQ"]
    NT = T // TT
    with bc.section() as st:
        wg = st.enter_context(nc.sbuf_tensor("rb_wg", [128, KB * 2048], BF16))
        wo = st.enter_context(nc.sbuf_tensor("rb_wo", [128, 16 * 1024], BF16))
        stage = [st.enter_context(nc.sbuf_tensor("rb_st0", [128, KB * TT], F32))]
        xbf = st.enter_context(nc.sbuf_tensor("rb_xb", [128, KB * TT], BF16))
        gsb = st.enter_context(nc.sbuf_tensor("rb_g", [128, 16 * TT], BF16))
        osb = [st.enter_context(nc.sbuf_tensor("rb_o0", [128, 16 * TT], BF16))]
        ofn = st.enter_context(nc.sbuf_tensor("rb_of", [128, 16 * TT], BF16))
        ysb = st.enter_context(nc.sbuf_tensor("rb_y", [128, KB * TT], F32))
        grow = st.enter_context(nc.sbuf_tensor("rb_gr", [4, TT], F32))
        rsbb = st.enter_context(nc.sbuf_tensor("rb_rb", [128, 4 * TT], BF16))
        gnw = st.enter_context(nc.sbuf_tensor("rb_gn", [128, 4], F32))
        zs7 = st.enter_context(nc.sbuf_tensor("rb_z7", [128, 7], BF16))
        zs7t = st.enter_context(nc.sbuf_tensor("rb_zt", [4, 4 * 128], F32))
        pq = [st.enter_context(nc.psum_tensor(f"rb_pq{i}", [128, 512], F32)) for i in range(2)]
        pst_ = st.enter_context(nc.psum_tensor("rb_ps", [128, 512], F32))
        prs = st.enter_context(nc.psum_tensor("rb_pr", [128, 512], F32))
        lnb = mk_lnbufs(bc, st, "rb", nc)

        bc.op("g", lambda e: e.dma_start(r3(wg, KB), wview(dram["rWgT"].ap())), writes=[wg], dma=True)
        bc.op("g", lambda e: e.dma_start(r3(wo, 16), wview(dram["rWoT"].ap())), writes=[wo], dma=True)
        bc.op("y", lambda e: e.dma_start(gnw[:, :], dram["gnw4"].ap()), writes=[gnw], dma=True)
        bc.op("y", lambda e: e.dma_start(zs7[:, :], dram["z7"].ap()), writes=[zs7], dma=True)
        bc.op("y", lambda e: e.dma_start(zs7t[:, :], dram["z7t"].ap()), writes=[zs7t], dma=True)
        w3o = r3(wo, 16)
        w3g = r3(wg, KB)
        for kb in range(16):
            bc.op("v", lambda e, kb=kb: e.tensor_scalar_mul(w3o[:, kb, :], w3o[:, kb, :],
                                                            gnw[:, kb % 4:kb % 4 + 1]),
                  reads=[wo, gnw], writes=[wo])
        g3 = r3(gsb, 16)
        of3 = r3(ofn, 16)
        oq3 = of3
        for seq in range(NSEQ):
            for ti in range(NT):
                gt = seq * NT + ti
                c0 = seq * T + ti * TT
                stg = stage[0]
                osl = osb[0]
                st3 = r3(stg, KB)
                o3 = r3(osl, 16)
                x3 = r3(xbf, KB)
                bc.op("y", lambda e, st3=st3, c0=c0: e.dma_start(st3, sview(dram["xT"].ap(), c0, c0 + TT)),
                      writes=[stg], dma=True)
                bc.op("g", lambda e, o3=o3, c0=c0: e.dma_start(o3, sview(dram["orawT"].ap(), c0, c0 + TT)),
                      writes=[osl], dma=True)
                bc.op("v", lambda e, x3=x3, st3=st3: e.tensor_copy(x3, st3), reads=[stg], writes=[xbf])
                for mb in range(16):
                    ps = pq[mb % 2]

                    def emg(e, mb=mb, x3=x3, ps=ps):
                        for kb in range(KB):
                            inst = e.matmul(ps[:, :], w3g[:, kb, mb * 128:(mb + 1) * 128], x3[:, kb, :],
                                            start=(kb == 0), stop=(kb == KB - 1))
                        return inst
                    bc.op("t", emg, reads=[wg, xbf], writes=[ps])
                    bc.op("s", lambda e, mb=mb, ps=ps: e.activation(g3[:, mb, :], ps[:, :], AF.Silu),
                          reads=[ps], writes=[(gsb.name, mb)])
                bc.op("v", lambda e, o3=o3: e.tensor_tensor(oq3, o3, o3, AL.mult),
                      reads=[osl], writes=[(ofn.name, "sq")])

                def em_ss(e):
                    for b in range(16):
                        jj = b // 4
                        inst = e.matmul(pst_[0:4, :], zs7[:, 3 - jj:7 - jj], oq3[:, b, :],
                                        start=(b == 0), stop=(b == 15))
                    return inst
                bc.op("t", em_ss, reads=[(ofn.name, "sq"), zs7], writes=[pst_])
                bc.op("s", lambda e: e.activation(grow[0:4, 0:TT], pst_[0:4, :], AF.Sqrt,
                                                  bias=EPS, scale=1.0 / DVR),
                      reads=[pst_, bc.g_fc], writes=[(grow.name, "sd")])
                bc.op("v", lambda e: e.reciprocal(grow[0:4, 0:TT], grow[0:4, 0:TT]),
                      reads=[(grow.name, "sd")], writes=[(grow.name, "r")])
                for h in range(4):
                    bc.op("t", lambda e, h=h: e.matmul(prs[:, :], zs7t[:, h * 128:(h + 1) * 128], grow[0:4, 0:TT],
                                                       start=True, stop=True),
                          reads=[(grow.name, "r"), zs7t], writes=[prs])
                    bc.op("s", lambda e, h=h: e.activation(rsbb[:, h * TT:(h + 1) * TT], prs[:, :], AF.Copy),
                          reads=[prs], writes=[(rsbb.name, h)])
                bc.op("v", lambda e, o3=o3: e.tensor_tensor(of3, o3, g3, AL.mult),
                      reads=[osl] + [(gsb.name, mb) for mb in range(16)], writes=[(ofn.name, "t1")])
                rs4 = rsbb.ap().rearrange("p (h t) -> p h t", h=4)
                ofh = ofn.ap().rearrange("p (h u t) -> p h u t", h=4, u=4)
                bc.op("v", lambda e, ofh=ofh: e.tensor_tensor(
                    ofh, ofh, rs4[:, :, None, :].broadcast_to([128, 4, 4, TT]), AL.mult),
                    reads=[(ofn.name, "t1")] + [(rsbb.name, h) for h in range(4)], writes=[ofn])
                for mb in range(KB):
                    ps = pq[mb % 2]

                    def emo(e, mb=mb, ps=ps):
                        for kb in range(16):
                            inst = e.matmul(ps[:, :], w3o[:, kb, mb * 128:(mb + 1) * 128], of3[:, kb, :],
                                            start=(kb == 0), stop=(kb == 15))
                        return inst
                    bc.op("t", emo, reads=[wo, ofn], writes=[ps])
                    bc.op("v", lambda e, mb=mb, ps=ps, st3=st3: e.tensor_tensor(
                        r3(ysb, KB)[:, mb, :], ps[:, :], st3[:, mb, :], AL.add),
                        reads=[ps, stg], writes=[(ysb.name, mb)])
                emit_ln(bc, lnb, r3(ysb, KB), [(ysb.name, mb) for mb in range(KB)], r3(ysb, KB), ysb)
                bc.op("s", lambda e, c0=c0: e.dma_start(sview(dram["xln1T"].ap(), c0, c0 + TT), r3(ysb, KB)),
                      reads=[ysb], dma=True)


def sec_final(bc, cfg, dram, src):
    nc = bc.nc
    T, NSEQ = cfg["T"], cfg["NSEQ"]
    NT = T // TT
    with bc.section() as st:
        sa = [st.enter_context(nc.sbuf_tensor(f"fn_a{i}", [128, KB * TT], F32)) for i in range(2)]
        sb_ = [st.enter_context(nc.sbuf_tensor(f"fn_b{i}", [128, KB * TT], F32)) for i in range(2)]
        ysb = st.enter_context(nc.sbuf_tensor("fn_y", [128, KB * TT], F32))
        lnb = mk_lnbufs(bc, st, "fn", nc)
        for seq in range(NSEQ):
            for ti in range(NT):
                gt = seq * NT + ti
                c0 = seq * T + ti * TT
                a, b = sa[gt % 2], sb_[gt % 2]
                a3, b3 = r3(a, KB), r3(b, KB)
                bc.op("y", lambda e, a3=a3, c0=c0: e.dma_start(a3, sview(dram["xln1T"].ap(), c0, c0 + TT)),
                      writes=[a], dma=True)
                bc.op("g", lambda e, b3=b3, c0=c0: e.dma_start(b3, sview(dram[src].ap(), c0, c0 + TT)),
                      writes=[b], dma=True)
                bc.op("v", lambda e, a3=a3, b3=b3: e.tensor_tensor(r3(ysb, KB), a3, b3, AL.add),
                      reads=[a, b], writes=[ysb])
                emit_ln(bc, lnb, r3(ysb, KB), [ysb], r3(ysb, KB), ysb)
                bc.op("s", lambda e, c0=c0: e.dma_start(sview(dram["outT"].ap(), c0, c0 + TT), r3(ysb, KB)),
                      reads=[ysb], dma=True)



def build_program(cfg):
    nc = bass.Bass("TRN2", target_bir_lowering=False)
    T, NSEQ, nlayer = cfg["T"], cfg["NSEQ"], cfg["nlayer"]
    TC = T * NSEQ
    dram = {}
    dram["xT"] = nc.dram_tensor("xT", [D, TC], F32, kind="ExternalInput")
    for n, shp, dt in [("rWqT", [D, 1024], F32), ("rWkT", [D, 1024], F32),
                       ("rWvT", [D, 2048], F32), ("rWgT", [D, 2048], F32),
                       ("rWoT", [2048, 1024], F32), ("gnw4", [128, 4], F32),
                       ("iWqkv", [6, D, 3072], F32), ("iWo", [6, D, 1024], F32),
                       ("iW1p", [6, D, 5632], F32), ("ib1p", [6, 128, 2, 22], F32),
                       ("iW2p", [6, 2816, D], F32), ("ib2c", [6, 128, 8], F32),
                       ("ilng", [6, 128, 8], F32), ("ilnb", [6, 128, 8], F32),
                       ("iqs", [128, 6], F32), ("iks", [128, 6], F32),
                       ("iqssw", [128, 6], F32), ("ikssw", [128, 6], F32),
                       ("ident", [128, 128], BF16), ("ident32", [128, 128], F32),
                       ("rcos", [128, T], F32), ("rsin", [128, T], F32),
                       ("maskT", [128, NHR, 2, CH], BF16), ("cdq", [128, NHR, CH], BF16),
                       ("kdec", [128, NHR, 2], F32),
                       ("iqa", [128, T], F32), ("iqb", [128, T], F32),
                       ("ika", [128, T], F32), ("ikb", [128, T], F32),
                       ("imask", [128, 4, 512], BF16), ("zl2", [128, 33], BF16),
                       ("bclhs", [16, KB * 128], BF16), ("z7", [128, 7], BF16),
                       ("z7t", [4, 4 * 128], F32), ("onescol", [128, 1], BF16),
                       ("onesrow32", [1, 128], F32), ("fconsts", [128, 1], F32)]:
        dram[n] = nc.dram_tensor(n, shp, dt, kind="ExternalInput")
    okind = "ExternalOutput" if cfg.get("dbg") else "Internal"
    dram["orawT"] = nc.dram_tensor("orawT", [2048, TC], BF16, kind=okind)
    dram["xln1T"] = nc.dram_tensor("xln1T", [D, TC], F32, kind=okind)
    dram["yA"] = nc.dram_tensor("yA", [D, TC], F32, kind=okind)
    dram["yB"] = nc.dram_tensor("yB", [D, TC], F32, kind=okind)
    dram["outT"] = nc.dram_tensor("outT", [D, TC], F32, kind="ExternalOutput")
    if cfg.get("dbg"):
        dram["qkdbgT"] = nc.dram_tensor("qkdbgT", [2048, TC], BF16, kind="ExternalOutput")
        dram["odbgT"] = nc.dram_tensor("odbgT", [D, TC], BF16, kind="ExternalOutput")

    bc = BC(nc)
    bc.g_ident = bc.es.enter_context(nc.sbuf_tensor("g_ident", [128, 128], BF16))
    bc.g_ident32 = bc.es.enter_context(nc.sbuf_tensor("g_ident32", [128, 128], F32))
    bc.g_ones1 = bc.es.enter_context(nc.sbuf_tensor("g_ones1", [128, 1], BF16))
    bc.g_onesr = bc.es.enter_context(nc.sbuf_tensor("g_onesr", [1, 128], F32))
    bc.g_imask = bc.es.enter_context(nc.sbuf_tensor("g_imask", [128, 4 * 512], BF16))
    bc.g_zl2 = bc.es.enter_context(nc.sbuf_tensor("g_zl2", [128, 33], BF16))
    bc.g_bclhs = bc.es.enter_context(nc.sbuf_tensor("g_bclhs", [16, KB * 128], BF16))
    bc.g_qs = bc.es.enter_context(nc.sbuf_tensor("g_qs", [128, 6], F32))
    bc.g_ks = bc.es.enter_context(nc.sbuf_tensor("g_ks", [128, 6], F32))
    bc.g_qssw = bc.es.enter_context(nc.sbuf_tensor("g_qssw", [128, 6], F32))
    bc.g_kssw = bc.es.enter_context(nc.sbuf_tensor("g_kssw", [128, 6], F32))
    bc.g_lng = bc.es.enter_context(nc.sbuf_tensor("g_lng", [128, 6 * 8], F32))
    bc.g_lnb = bc.es.enter_context(nc.sbuf_tensor("g_lnb", [128, 6 * 8], F32))
    bc.g_b2 = bc.es.enter_context(nc.sbuf_tensor("g_b2", [128, 6 * 8], F32))
    bc.g_fc = bc.es.enter_context(nc.sbuf_tensor("g_fc", [128, 1], F32))
    bc.op("y", lambda e: e.dma_start(bc.g_fc[:, :], dram["fconsts"].ap()), writes=[bc.g_fc], dma=True)
    nc.const_aps.register(F32, EPS, bc.g_fc[:, 0:1])
    bc.op("y", lambda e: e.dma_start(bc.g_ident[:, :], dram["ident"].ap()), writes=[bc.g_ident], dma=True)
    bc.op("y", lambda e: e.dma_start(bc.g_ident32[:, :], dram["ident32"].ap()), writes=[bc.g_ident32], dma=True)
    bc.op("y", lambda e: e.dma_start(bc.g_ones1[:, :], dram["onescol"].ap()), writes=[bc.g_ones1], dma=True)
    bc.op("y", lambda e: e.dma_start(bc.g_onesr[:, :], dram["onesrow32"].ap()), writes=[bc.g_onesr], dma=True)
    bc.op("y", lambda e: e.dma_start(r3(bc.g_imask, 4), dram["imask"].ap()), writes=[bc.g_imask], dma=True)
    bc.op("y", lambda e: e.dma_start(bc.g_zl2[:, :], dram["zl2"].ap()), writes=[bc.g_zl2], dma=True)
    bc.op("y", lambda e: e.dma_start(bc.g_bclhs[:, :], dram["bclhs"].ap()), writes=[bc.g_bclhs], dma=True)
    for buf, nm in ((bc.g_qs, "iqs"), (bc.g_ks, "iks"), (bc.g_qssw, "iqssw"), (bc.g_kssw, "ikssw")):
        bc.op("y", lambda e, buf=buf, nm=nm: e.dma_start(buf[:, :], dram[nm].ap()), writes=[buf], dma=True)
    for buf, nm in ((bc.g_lng, "ilng"), (bc.g_lnb, "ilnb"), (bc.g_b2, "ib2c")):
        bc.op("y", lambda e, buf=buf, nm=nm: e.dma_start(
            buf.ap().rearrange("p (l c) -> p l c", l=6),
            dram[nm].ap().rearrange("l p c -> p l c")), writes=[buf], dma=True)

    sec_retnet_A(bc, cfg, dram)
    sec_retnet_BC(bc, cfg, dram)
    cur = "xln1T"
    for l in range(nlayer):
        nxt = "yA" if cur in ("xln1T", "yB") else "yB"
        sec_attn(bc, cfg, dram, l, cur, nxt)
        if cfg.get("attn_only"):
            cur = nxt
            break
        cur2 = "yB" if nxt == "yA" else "yA"
        sec_ffn(bc, cfg, dram, l, nxt, cur2)
        cur = cur2
    if not cfg.get("attn_only"):
        sec_final(bc, cfg, dram, cur)
    bc.flush()
    return nc


def sec_attn(bc, cfg, dram, l, src, dst):
    nc = bc.nc
    T, NSEQ = cfg["T"], cfg["NSEQ"]
    NT = T // TT
    with bc.section() as st:
        wqkv = st.enter_context(nc.sbuf_tensor(f"at{l}_wqkv", [128, KB * 3072], BF16))
        wo = st.enter_context(nc.sbuf_tensor(f"at{l}_wo", [128, KB * 1024], BF16))
        stage = [st.enter_context(nc.sbuf_tensor(f"at{l}_st{i}", [128, KB * (TT + 1)], F32)) for i in range(2)]
        xsh = st.enter_context(nc.sbuf_tensor(f"at{l}_xs", [128, KB * TT], BF16))
        qk = st.enter_context(nc.sbuf_tensor(f"at{l}_qk", [128, 16 * TT], BF16))
        sqb = st.enter_context(nc.sbuf_tensor(f"at{l}_sq", [128, KB * TT], BF16))
        tabs = st.enter_context(nc.sbuf_tensor(f"at{l}_tb", [128, 2 * TT], F32))
        rows = st.enter_context(nc.sbuf_tensor(f"at{l}_rw", [16, 2 * TT], F32))
        rsqT = st.enter_context(nc.sbuf_tensor(f"at{l}_rqT", [128, 4 * 16], F32))
        rowsb = st.enter_context(nc.sbuf_tensor(f"at{l}_rwb", [16, TT], BF16))
        kring = st.enter_context(nc.sbuf_tensor(f"at{l}_kr", [128, KB * 2 * 256], BF16))
        vring = st.enter_context(nc.sbuf_tensor(f"at{l}_vr", [128, 6 * 1024], BF16))
        ssb = st.enter_context(nc.sbuf_tensor(f"at{l}_S", [128, 4 * 512], F32))
        psb = st.enter_context(nc.sbuf_tensor(f"at{l}_P", [128, 16 * 512], BF16))
        ptsb = st.enter_context(nc.sbuf_tensor(f"at{l}_PT", [128, 2 * 4 * 256], BF16))
        lbuf = st.enter_context(nc.sbuf_tensor(f"at{l}_lb", [128, 64], F32))
        rlb = st.enter_context(nc.sbuf_tensor(f"at{l}_rl", [128, 64], F32))
        osb = st.enter_context(nc.sbuf_tensor(f"at{l}_O", [128, KB * TT], BF16))
        mng = st.enter_context(nc.sbuf_tensor(f"at{l}_mg", [128, 8], F32))
        pb = [st.enter_context(nc.psum_tensor(f"at{l}_pb{i}", [128, 512], F32)) for i in range(4)]
        pPT = st.enter_context(nc.psum_tensor(f"at{l}_pPT", [128, 1024], BF16))
        pw = [st.enter_context(nc.psum_tensor(f"at{l}_pw{i}", [128, 512], F32)) for i in range(2)]
        po_ = pw

        ident = bc.g_ident
        im3 = r3(bc.g_imask, 4)
        w3 = r3(wqkv, KB)
        wo3 = r3(wo, KB)
        qk3 = r3(qk, 16)
        sq3 = r3(sqb, KB)
        tb3 = r3(tabs, 2)
        rqT3 = r3(rsqT, 4)
        kr3 = kring.ap().rearrange("p (b w c) -> p b w c", b=KB, w=2)
        vr3 = r3(vring, 6)
        S2 = r3(ssb, 4)
        P2 = r3(psb, 16)
        PT3 = ptsb.ap().rearrange("p (c s q) -> p c s q", c=2, s=4)
        pPT3 = pPT.ap().rearrange("p (s q) -> p s q", s=4)

        bc.op("g", lambda e: e.dma_start(w3, wview(dram["iWqkv"].ap()[l])), writes=[wqkv], dma=True)
        bc.op("g", lambda e: e.dma_start(wo3, wview(dram["iWo"].ap()[l])), writes=[wo], dma=True)
        bc.op("g", lambda e: e.memset(kring[:, :], 0.0), writes=[kring])
        for c_ in range(2):
            bc.op("g", lambda e, c_=c_: e.memset(PT3[:, c_, 3, 0:128], 0.0), writes=[(ptsb.name, "z", c_, 0)])
            bc.op("g", lambda e, c_=c_: e.memset(PT3[:, c_, 0, 128:256], 0.0), writes=[(ptsb.name, "z", c_, 1)])
        bc.op("g", lambda e: e.memset(vring[:, :], 0.0), writes=[vring])

        for seq in range(NSEQ):
            for ti in range(NT):
                gt = seq * NT + ti
                c0 = seq * T + ti * TT
                stg = stage[gt % 2]
                st3 = stg.ap().rearrange("p (kb t) -> p kb t", kb=KB)
                if ti == 0:
                    bc.op("y", lambda e, st3=st3, c0=c0: e.dma_start(
                        st3[:, :, 1:TT + 1], sview(dram[src].ap(), c0, c0 + TT)), writes=[stg], dma=True)
                    bc.op("g", lambda e, st3=st3: e.memset(st3[:, :, 0:1], 0.0), writes=[(stg.name, "z")])
                else:
                    bc.op("y", lambda e, st3=st3, c0=c0: e.dma_start(
                        st3, sview(dram[src].ap(), c0 - 1, c0 + TT)), writes=[stg], dma=True)
                stkeys = [stg, (stg.name, "z")] if ti == 0 else [stg]
                tc0 = ti * TT
                x3s = xsh.ap().rearrange("p (kb t) -> p kb t", kb=KB)
                bc.op("v", lambda e, st3=st3, x3s=x3s: e.tensor_copy(x3s[:, 0:4, :], st3[:, 0:4, 1:TT + 1]),
                      reads=stkeys, writes=[(xsh.name, 0)])
                bc.op("v", lambda e, st3=st3, x3s=x3s: e.tensor_copy(x3s[:, 4:8, :], st3[:, 4:8, 0:TT]),
                      reads=stkeys, writes=[(xsh.name, 1)])
                xkeys = [(xsh.name, 0), (xsh.name, 1)]
                for mb in range(16):
                    ps = pb[mb % 2]

                    def em(e, mb=mb, ps=ps, x3s=x3s):
                        for kb in range(KB):
                            inst = e.matmul(ps[:, :], w3[:, kb, mb * 128:(mb + 1) * 128], x3s[:, kb, :],
                                            start=(kb == 0), stop=(kb == KB - 1))
                        return inst
                    bc.op("t", em, reads=[wqkv] + xkeys, writes=[ps])
                    bc.op("s", lambda e, mb=mb, ps=ps: e.activation(qk3[:, mb, :], ps[:, :], AF.Copy),
                          reads=[ps], writes=[(qk.name, mb)])
                for ts_ in range(4):
                    for nch in range(2):
                        ps = pb[(ts_ * 2 + nch) % 2]
                        sv = (ti * 4 + ts_) % 6

                        def emv(e, ts_=ts_, nch=nch, ps=ps, x3s=x3s):
                            for kb in range(KB):
                                inst = e.matmul(ps[:, :], x3s[:, kb, ts_ * 128:(ts_ + 1) * 128],
                                                w3[:, kb, 2048 + nch * 512:2048 + (nch + 1) * 512],
                                                start=(kb == 0), stop=(kb == KB - 1))
                            return inst
                        bc.op("t", emv, reads=[wqkv] + xkeys, writes=[ps])
                        bc.op("s", lambda e, sv=sv, nch=nch, ps=ps: e.activation(
                            vr3[:, sv, nch * 512:(nch + 1) * 512], ps[:, :], AF.Copy),
                            reads=[ps], writes=[(vring.name, sv * 2 + nch)])
                for half, base, rcol in ((0, 0, 0), (1, 8, TT)):
                    bc.op("v", lambda e, base=base: e.tensor_tensor(sq3, qk3[:, base:base + 8, :],
                                                                    qk3[:, base:base + 8, :], AL.mult),
                          reads=[(qk.name, base + b) for b in range(8)], writes=[sqb])
                    ps = pb[2]

                    def emss(e, ps=ps):
                        for b in range(KB):
                            inst = e.matmul(ps[0:16, :], bc.g_zl2[:, 16 - 2 * b:32 - 2 * b], sq3[:, b, :],
                                            start=(b == 0), stop=(b == KB - 1))
                        return inst
                    bc.op("t", emss, reads=[sqb, bc.g_zl2], writes=[ps])
                    if half == 0:
                        bc.op("s", lambda e, ps=ps, rcol=rcol: e.activation(
                            rows[0:16, rcol:rcol + TT], ps[0:16, :], AF.Sqrt, scale=1.0 / 64.0),
                            reads=[ps], writes=[(rows.name, half)])
                        bc.op("v", lambda e, rcol=rcol: e.tensor_scalar_max(
                            rows[0:16, rcol:rcol + TT], rows[0:16, rcol:rcol + TT], 1.25e-13),
                            reads=[(rows.name, half)], writes=[(rows.name, half)])
                    else:
                        bc.op("s", lambda e, ps=ps, rcol=rcol: e.activation(
                            rows[0:16, rcol:rcol + TT], ps[0:16, :], AF.Sqrt),
                            reads=[ps], writes=[(rows.name, half)])
                        bc.op("v", lambda e, rcol=rcol: e.tensor_scalar_max(
                            rows[0:16, rcol:rcol + TT], rows[0:16, rcol:rcol + TT], 1e-12),
                            reads=[(rows.name, half)], writes=[(rows.name, half)])
                    bc.op("v", lambda e, rcol=rcol: e.reciprocal(
                        rows[0:16, rcol:rcol + TT], rows[0:16, rcol:rcol + TT]),
                        reads=[(rows.name, half)], writes=[(rows.name, half)])
                def emrt(e):
                    for sub in range(4):
                        inst = e.transpose(pb[3][:, sub * 16:(sub + 1) * 16],
                                           rows[0:16, sub * 128:(sub + 1) * 128], bc.g_ident32[0:16, 0:16])
                    return inst
                bc.op("t", emrt, reads=[(rows.name, 0), bc.g_ident32], writes=[pb[3]])
                bc.op("s", lambda e: e.activation(rsqT[:, :], pb[3][:, 0:64], AF.Copy),
                      reads=[pb[3]], writes=[rsqT])
                for half, base, ta, tb_ in ((0, 0, 0, 1), (1, 8, 0, 1)):
                    hkeys = [(qk.name, base + b) for b in range(8)]
                    nms = ("iqa", "iqb") if half == 0 else ("ika", "ikb")
                    scs = (bc.g_qs, bc.g_qssw) if half == 0 else (bc.g_ks, bc.g_kssw)
                    for k_, (nm, sc) in enumerate(zip(nms, scs)):
                        bc.op("y", lambda e, k_=k_, nm=nm, tc0=tc0: e.dma_start(
                            tb3[:, k_, :], dram[nm].ap()[:, tc0:tc0 + TT]),
                            writes=[(tabs.name, k_)], dma=True)
                        bc.op("v", lambda e, k_=k_, sc=sc: e.tensor_scalar_mul(
                            tb3[:, k_, :], tb3[:, k_, :], sc[:, l:l + 1]),
                            reads=[(tabs.name, k_), sc], writes=[(tabs.name, k_)])
                    for qn, (o0, i0) in enumerate(((0, 32), (32, 0), (64, 96), (96, 64))):
                        bc.op("y", lambda e, o0=o0, i0=i0, base=base: e.dma_start(
                            sq3[o0:o0 + 32, :, :], qk3[i0:i0 + 32, base:base + 8, :]),
                            reads=hkeys, writes=[(sqb.name, qn)], dma=True)
                    swkeys = [(sqb.name, qn) for qn in range(4)]
                    tbb = tb3[:, ta, None, :].broadcast_to([128, 8, TT])
                    tbs = tb3[:, tb_, None, :].broadcast_to([128, 8, TT])
                    bc.op("v", lambda e, base=base, tbb=tbb: e.tensor_tensor(
                        qk3[:, base:base + 8, :], qk3[:, base:base + 8, :], tbb, AL.mult),
                        reads=hkeys + swkeys + [(tabs.name, ta)], writes=hkeys)
                    bc.op("v", lambda e, tbs=tbs: e.tensor_tensor(sq3, sq3, tbs, AL.mult),
                          reads=swkeys + [(tabs.name, tb_)], writes=[sqb])
                    bc.op("v", lambda e, base=base: e.tensor_tensor(
                        qk3[:, base:base + 8, :], qk3[:, base:base + 8, :], sq3, AL.add),
                        reads=hkeys + [sqb], writes=hkeys)
                bc.op("v", lambda e: e.tensor_copy(rowsb[0:16, :], rows[0:16, TT:2 * TT]),
                      reads=[(rows.name, 1)], writes=[rowsb])
                for b in range(KB):
                    ps = pb[2 + (b % 2)]
                    bc.op("t", lambda e, b=b, ps=ps: e.matmul(
                        ps[:, :], bc.g_bclhs[0:16, b * 128:(b + 1) * 128], rowsb[0:16, :],
                        start=True, stop=True),
                        reads=[rowsb, bc.g_bclhs], writes=[ps])
                    bc.op("v", lambda e, b=b, ps=ps: e.tensor_tensor(
                        qk3[:, 8 + b, :], qk3[:, 8 + b, :], ps[:, :], AL.mult),
                        reads=[(qk.name, 8 + b), ps], writes=[(qk.name, 8 + b)])
                for wl in range(2):
                    wg = ti * 2 + wl
                    w0 = (wg == 0)
                    pprev = (wg - 1) % 2
                    pown = wg % 2
                    kkeys = [(qk.name, 8 + b) for b in range(8)]
                    bc.op("v", lambda e, wl=wl, pown=pown: e.tensor_copy(
                        kr3[:, :, pown, :], qk3[:, 8:16, wl * 256:(wl + 1) * 256]),
                        reads=kkeys, writes=[(kring.name, pown)])
                    def emit_S(hp, wl=wl, w0=w0, pprev=pprev, pown=pown):
                        par4 = hp % 4
                        for qt in range(2):
                            qc0 = wl * 256 + qt * 128
                            sub = wl * 2 + qt
                            var = qt + (2 if w0 else 0)
                            lo, hi = (0, 384) if qt == 0 else (128, 512)
                            bq = 2 * qt

                            def emS(e, hp=hp, qt=qt, qc0=qc0, pprev=pprev, pown=pown, bq=bq, var=var,
                                    lo=lo, hi=hi):
                                for half, par in ((0, pprev), (1, pown)):
                                    a = max(lo, half * 256) - half * 256
                                    z = min(hi, (half + 1) * 256) - half * 256
                                    for c, (r0, tp) in enumerate(((0, (0, 0)), (64, (64, 0)))):
                                        e.matmul(pb[bq + c][:, half * 256 + a:half * 256 + z],
                                                 qk3[r0:r0 + 64, hp, qc0:qc0 + 128],
                                                 kr3[r0:r0 + 64, hp, par, a:z],
                                                 start=(half == 0), stop=False,
                                                 tile_position=tp)
                                for c in range(2):
                                    inst = e.matmul(pb[bq + c][:, lo:hi], ident[:, :], im3[:, var, lo:hi],
                                                    start=False, stop=True)
                                return inst
                            bc.op("t", emS, reads=[(qk.name, hp), (kring.name, pprev), (kring.name, pown),
                                                   ident, bc.g_imask],
                                  writes=[pb[bq], pb[bq + 1]])
                            for c in range(2):
                                bc.op("v", lambda e, c=c, bq=bq, lo=lo, hi=hi: e.tensor_reduce(
                                    mng[:, bq + c:bq + c + 1], pb[bq + c][:, lo:hi],
                                    AX.X, AL.max, negate=True),
                                    reads=[pb[bq + c]], writes=[(mng.name, bq + c)])
                            bc.op("v", lambda e, bq=bq, sub=sub, hp=hp: e.tensor_tensor(
                                mng[:, 4 + bq:6 + bq], mng[:, bq:bq + 2],
                                rqT3[:, sub, 2 * hp:2 * hp + 2], AL.mult),
                                reads=[(mng.name, bq), (mng.name, bq + 1), rsqT],
                                writes=[(mng.name, "s", bq), (mng.name, "s", bq + 1)])
                            for c in range(2):
                                h = 2 * hp + c
                                col = sub * 16 + h
                                bc.op("s", lambda e, c=c, qt=qt, col=col, bq=bq, sub=sub, h=h,
                                      par4=par4, lo=lo, hi=hi: e.activation(
                                    P2[:, 4 * par4 + qt * 2 + c, lo:hi], pb[bq + c][:, lo:hi], AF.Exp,
                                    scale=rqT3[:, sub, h:h + 1],
                                    bias=mng[:, 4 + bq + c:4 + bq + c + 1],
                                    accum_out=lbuf[:, col:col + 1]),
                                    reads=[pb[bq + c], (mng.name, "s", bq + c), rsqT],
                                    writes=[(psb.name, 4 * par4 + qt * 2 + c), (lbuf.name, col)])
                    def emit_tail(hp, wl=wl, wg=wg):
                        lcols = [(wl * 2 + qt) * 16 + 2 * hp + c for qt in range(2) for c in range(2)]
                        l4 = lbuf.ap().rearrange("p (a b) -> p a b", a=4)
                        r4 = rlb.ap().rearrange("p (a b) -> p a b", a=4)
                        bc.op("v", lambda e, wl=wl, hp=hp, l4=l4, r4=r4: e.reciprocal(
                            r4[:, wl * 2:wl * 2 + 2, 2 * hp:2 * hp + 2],
                            l4[:, wl * 2:wl * 2 + 2, 2 * hp:2 * hp + 2]),
                            reads=[(lbuf.name, cc) for cc in lcols], writes=[(rlb.name, wl, hp)])
                        par4 = hp % 4
                        for qt in range(2):
                            for c in range(2):
                                col = (wl * 2 + qt) * 16 + 2 * hp + c
                                lo, hi = (0, 384) if qt == 0 else (128, 512)
                                bc.op("v", lambda e, qt=qt, c=c, col=col, par4=par4, lo=lo, hi=hi: e.tensor_scalar_mul(
                                    P2[:, 4 * par4 + qt * 2 + c, lo:hi], P2[:, 4 * par4 + qt * 2 + c, lo:hi],
                                    rlb[:, col:col + 1]),
                                    reads=[(psb.name, 4 * par4 + qt * 2 + c), (rlb.name, wl, hp)],
                                    writes=[(psb.name, 4 * par4 + qt * 2 + c)])
                        for c in range(2):
                            def emPT(e, c=c, par4=par4):
                                for qt in range(2):
                                    for s_ in ((0, 1, 2) if qt == 0 else (1, 2, 3)):
                                        inst = e.transpose(pPT3[:, s_, qt * 128:(qt + 1) * 128],
                                                           P2[:, 4 * par4 + qt * 2 + c, s_ * 128:(s_ + 1) * 128],
                                                           ident[:, :])
                                return inst
                            bc.op("t", emPT, reads=[(psb.name, 4 * par4 + c), (psb.name, 4 * par4 + 2 + c), ident],
                                  writes=[pPT])
                            bc.op("v", lambda e, c=c: e.tensor_copy(PT3[:, c, 0:3, 0:128], pPT3[:, 0:3, 0:128]),
                                  reads=[pPT], writes=[(ptsb.name, c)], inc=False)
                            bc.op("v", lambda e, c=c: e.tensor_copy(PT3[:, c, 1:4, 128:256], pPT3[:, 1:4, 128:256]),
                                  reads=[pPT], writes=[(ptsb.name, c)])
                        def emO(e, hp=hp, wg=wg):
                            for s_ in range(4):
                                sv = (2 * (wg - 1) + s_) % 6
                                for c in range(2):
                                    h = 2 * hp + c
                                    inst = e.matmul(po_[c][c * 64:(c + 1) * 64, 0:256],
                                                    vr3[:, sv, h * 64:(h + 1) * 64],
                                                    PT3[:, c, s_, :],
                                                    start=(s_ == 0), stop=(s_ == 3),
                                                    tile_position=(0, c * 64))
                            return inst
                        vkeys = [(vring.name, ((2 * (wg - 1) + s_) % 6) * 2 + nch) for s_ in range(4) for nch in range(2)]
                        bc.op("t", emO, reads=[(ptsb.name, 0), (ptsb.name, 1)] + vkeys, writes=[po_[0], po_[1]])
                        o3w_ = osb.ap().rearrange("p (b t) -> p b t", b=KB)
                        for c in range(2):
                            bc.op("s", lambda e, hp=hp, wl=wl, c=c, o3w_=o3w_: e.activation(
                                o3w_[c * 64:(c + 1) * 64, hp, wl * 256:(wl + 1) * 256],
                                po_[c][c * 64:(c + 1) * 64, 0:256], AF.Copy),
                                reads=[po_[c]], writes=[(osb.name, hp, wl, c)])

                    for hp in range(KB):
                        emit_S(hp)
                        if hp >= 2:
                            emit_tail(hp - 2)
                    emit_tail(KB - 2)
                    emit_tail(KB - 1)
                    o3w = osb.ap().rearrange("p (b t) -> p b t", b=KB)
                    okeys = [(osb.name, hp, wl, c) for hp in range(KB) for c in range(2)]
                    for mb in range(KB):
                        ps = pw[mb % 2]

                        def emWo(e, mb=mb, ps=ps, wl=wl):
                            for kb in range(KB):
                                inst = e.matmul(ps[:, 0:256], wo3[:, kb, mb * 128:(mb + 1) * 128],
                                                o3w[:, kb, wl * 256:(wl + 1) * 256],
                                                start=(kb == 0), stop=(kb == KB - 1))
                            return inst
                        bc.op("t", emWo, reads=[wo] + okeys, writes=[ps])
                        bc.op("v", lambda e, mb=mb, ps=ps, st3=st3, wl=wl: e.tensor_tensor(
                            st3[:, mb, 1 + wl * 256:1 + (wl + 1) * 256], ps[:, 0:256],
                            st3[:, mb, 1 + wl * 256:1 + (wl + 1) * 256], AL.add),
                            reads=[ps] + stkeys, writes=[(stg.name, "y", mb, wl)])
                if cfg.get("dbg"):
                    bc.op("g", lambda e, c0=c0: e.dma_start(
                        sview(dram["qkdbgT"].ap(), c0, c0 + TT), qk3),
                        reads=[(qk.name, m_) for m_ in range(16)], dma=True)
                    bc.op("g", lambda e, c0=c0: e.dma_start(
                        sview(dram["odbgT"].ap(), c0, c0 + TT),
                        osb.ap().rearrange("p (b t) -> p b t", b=KB)),
                        reads=[(osb.name, hp_, wl_, c_) for hp_ in range(KB) for wl_ in range(2) for c_ in range(2)],
                        dma=True)
                ykeys = [(stg.name, "y", mb, wl) for mb in range(KB) for wl in range(2)]
                bc.op("s", lambda e, st3=st3, c0=c0: e.dma_start(
                    sview(dram[dst].ap(), c0, c0 + TT), st3[:, :, 1:TT + 1]),
                    reads=ykeys, dma=True)



def sec_ffn(bc, cfg, dram, l, src, dst):
    nc = bc.nc
    T, NSEQ = cfg["T"], cfg["NSEQ"]
    NT = T // TT
    NCB = 22
    with bc.section() as st:
        w1 = st.enter_context(nc.sbuf_tensor(f"ff{l}_w1", [128, KB * 5632], BF16))
        w2 = st.enter_context(nc.sbuf_tensor(f"ff{l}_w2", [128, NCB * 1024], BF16))
        stage = st.enter_context(nc.sbuf_tensor(f"ff{l}_st", [128, KB * (TT + 1)], F32))
        xnbf = st.enter_context(nc.sbuf_tensor(f"ff{l}_xn", [128, KB * TT], BF16))
        gc = st.enter_context(nc.sbuf_tensor(f"ff{l}_gc", [128, NCB * TT], BF16))
        atmp = st.enter_context(nc.sbuf_tensor(f"ff{l}_a", [128, TT], BF16))
        gel = st.enter_context(nc.sbuf_tensor(f"ff{l}_g", [128, TT], BF16))
        btot = st.enter_context(nc.sbuf_tensor(f"ff{l}_bt", [128, 2 * NCB], F32))
        b1sb = st.enter_context(nc.sbuf_tensor(f"ff{l}_b1", [128, 2 * NCB], F32))
        lnbb = st.enter_context(nc.sbuf_tensor(f"ff{l}_lb", [128, KB], BF16))
        rows = st.enter_context(nc.sbuf_tensor(f"ff{l}_rw", [1, 4 * TT], F32))
        pa = st.enter_context(nc.psum_tensor(f"ff{l}_pa", [128, 512], F32))
        pg = st.enter_context(nc.psum_tensor(f"ff{l}_pg", [128, 512], F32))
        pw = [st.enter_context(nc.psum_tensor(f"ff{l}_pw{i}", [128, 512], F32)) for i in range(2)]
        pln = st.enter_context(nc.psum_tensor(f"ff{l}_pln", [128, 512], F32))
        pl2 = st.enter_context(nc.psum_tensor(f"ff{l}_pl2", [128, 512], F32))
        pbc0 = st.enter_context(nc.psum_tensor(f"ff{l}_pb0", [128, 512], F32))
        pbc1 = st.enter_context(nc.psum_tensor(f"ff{l}_pb1", [128, 512], F32))

        w13 = r3(w1, KB)
        w23 = r3(w2, NCB)
        bt3 = r3(btot, 2)
        b13 = r3(b1sb, 2)
        xn3 = r3(xnbf, KB)
        gc3 = r3(gc, NCB)
        ones1 = bc.g_ones1
        onesr = bc.g_onesr

        bc.op("g", lambda e: e.dma_start(w13, wview(dram["iW1p"].ap()[l])), writes=[w1], dma=True)
        bc.op("g", lambda e: e.dma_start(w23, wview(dram["iW2p"].ap()[l])), writes=[w2], dma=True)
        bc.op("y", lambda e: e.dma_start(
            b13, dram["ib1p"].ap()[l]), writes=[b1sb], dma=True)
        bc.op("v", lambda e: e.tensor_copy(
            lnbb[:, :], bc.g_lnb.ap().rearrange("p (a c) -> p a c", a=6)[:, l, :]),
            reads=[bc.g_lnb], writes=[lnbb])

        def emvb(e):
            for half in range(2):
                for cb in range(NCB):
                    for kb in range(KB):
                        inst = e.matmul(pa[:, half * NCB + cb:half * NCB + cb + 1],
                                        w13[:, kb, (half * NCB + cb) * 128:(half * NCB + cb + 1) * 128],
                                        lnbb[:, kb:kb + 1], start=(kb == 0), stop=(kb == KB - 1))
            return inst
        bc.op("t", emvb, reads=[w1, lnbb], writes=[pa])
        bc.op("v", lambda e: e.tensor_tensor(btot[:, :], pa[:, 0:2 * NCB], b1sb[:, :], AL.add),
              reads=[pa, b1sb], writes=[btot])
        for kb in range(KB):
            bc.op("v", lambda e, kb=kb: e.tensor_scalar_mul(
                w13[:, kb, :], w13[:, kb, :],
                bc.g_lng.ap().rearrange("p (a c) -> p a c", a=6)[:, l, kb:kb + 1]),
                reads=[w1, bc.g_lng], writes=[w1])

        st3 = r3(stage, KB)
        for seq in range(NSEQ):
            for ti in range(NT):
                c0 = seq * T + ti * TT
                if ti == 0:
                    bc.op("y", lambda e, c0=c0: e.dma_start(
                        st3[:, :, 1:TT + 1], sview(dram[src].ap(), c0, c0 + TT)), writes=[stage], dma=True)
                    bc.op("g", lambda e: e.memset(st3[:, :, 0:1], 0.0), writes=[(stage.name, "z")])
                else:
                    bc.op("y", lambda e, c0=c0: e.dma_start(
                        st3, sview(dram[src].ap(), c0 - 1, c0 + TT)), writes=[stage], dma=True)
                stkeys = [stage, (stage.name, "z")] if ti == 0 else [stage]
                bc.op("v", lambda e: e.tensor_copy(xn3[:, 0:4, :], st3[:, 0:4, 1:TT + 1]),
                      reads=stkeys, writes=[(xnbf.name, 0)])
                bc.op("v", lambda e: e.tensor_copy(xn3[:, 4:8, :], st3[:, 4:8, 0:TT]),
                      reads=stkeys, writes=[(xnbf.name, 1)])
                xkeys = [(xnbf.name, 0), (xnbf.name, 1)]

                def s1(e):
                    for kb in range(KB):
                        inst = e.matmul(pln[0:1, :], ones1[:, :], xn3[:, kb, :],
                                        start=(kb == 0), stop=(kb == KB - 1))
                    return inst
                bc.op("t", s1, reads=xkeys + [ones1], writes=[pln])
                bc.op("v", lambda e: e.tensor_tensor(xn3, xn3, xn3, AL.mult), reads=xkeys, writes=[xnbf])

                def s2(e):
                    for kb in range(KB):
                        inst = e.matmul(pl2[0:1, :], ones1[:, :], xn3[:, kb, :],
                                        start=(kb == 0), stop=(kb == KB - 1))
                    return inst
                bc.op("t", s2, reads=[xnbf, ones1], writes=[pl2])

                def rsl(k):
                    return rows[0:1, k * TT:(k + 1) * TT]
                bc.op("s", lambda e: e.activation(rsl(0), pln[0:1, :], AF.Copy, scale=1.0 / D),
                      reads=[pln], writes=[(rows.name, "mu")])
                bc.op("s", lambda e: e.activation(rsl(1), pl2[0:1, :], AF.Copy, scale=1.0 / D),
                      reads=[pl2], writes=[(rows.name, "sq")])
                bc.op("v", lambda e: e.tensor_tensor(rsl(2), rsl(0), rsl(0), AL.mult),
                      reads=[(rows.name, "mu")], writes=[(rows.name, "mu2")])
                bc.op("v", lambda e: e.tensor_tensor(rsl(1), rsl(1), rsl(2), AL.subtract),
                      reads=[(rows.name, "sq"), (rows.name, "mu2")], writes=[(rows.name, "var")])
                bc.op("s", lambda e: e.activation(rsl(1), rsl(1), AF.Sqrt, bias=EPS),
                      reads=[(rows.name, "var"), bc.g_fc], writes=[(rows.name, "sd")])
                bc.op("v", lambda e: e.reciprocal(rsl(1), rsl(1)),
                      reads=[(rows.name, "sd")], writes=[(rows.name, "rstd")])
                bc.op("v", lambda e: e.tensor_tensor(rsl(2), rsl(0), rsl(1), AL.mult),
                      reads=[(rows.name, "mu"), (rows.name, "rstd")], writes=[(rows.name, "m2")])
                bc.op("t", lambda e: e.matmul(pbc0[:, :], onesr[:, :], rsl(2), start=True, stop=True),
                      reads=[(rows.name, "m2"), onesr], writes=[pbc0])
                bc.op("t", lambda e: e.matmul(pbc1[:, :], onesr[:, :], rsl(1), start=True, stop=True),
                      reads=[(rows.name, "rstd"), onesr], writes=[pbc1])
                rb1 = pbc1[:, None, :].broadcast_to([128, 4, TT])
                bc.op("v", lambda e, rb1=rb1: e.tensor_tensor(xn3[:, 0:4, :], st3[:, 0:4, 1:TT + 1], rb1, AL.mult),
                      reads=stkeys + [pbc1], writes=[(xnbf.name, 0)])
                bc.op("v", lambda e, rb1=rb1: e.tensor_tensor(xn3[:, 4:8, :], st3[:, 4:8, 0:TT], rb1, AL.mult),
                      reads=stkeys + [pbc1], writes=[(xnbf.name, 1)])
                rb0 = pbc0[:, None, :].broadcast_to([128, KB, TT])
                bc.op("v", lambda e, rb0=rb0: e.tensor_tensor(xn3, xn3, rb0, AL.subtract),
                      reads=xkeys + [pbc0], writes=[(xnbf.name, 0), (xnbf.name, 1)])
                for cb in range(NCB):
                    def ema(e, cb=cb):
                        for kb in range(KB):
                            inst = e.matmul(pa[:, :], w13[:, kb, cb * 128:(cb + 1) * 128], xn3[:, kb, :],
                                            start=(kb == 0), stop=(kb == KB - 1))
                        return inst
                    bc.op("t", ema, reads=[w1] + xkeys, writes=[pa])

                    def emg(e, cb=cb):
                        for kb in range(KB):
                            inst = e.matmul(pg[:, :], w13[:, kb, (NCB + cb) * 128:(NCB + cb + 1) * 128],
                                            xn3[:, kb, :], start=(kb == 0), stop=(kb == KB - 1))
                        return inst
                    bc.op("t", emg, reads=[w1] + xkeys, writes=[pg])
                    bc.op("s", lambda e, cb=cb: e.activation(atmp[:, :], pa[:, :], AF.Identity,
                                                             bias=bt3[:, 0, cb:cb + 1]),
                          reads=[pa, btot], writes=[atmp])
                    bc.op("s", lambda e, cb=cb: e.activation(gel[:, :], pg[:, :], AF.Gelu,
                                                             bias=bt3[:, 1, cb:cb + 1]),
                          reads=[pg, btot], writes=[gel])
                    bc.op("v", lambda e, cb=cb: e.tensor_tensor(gc3[:, cb, :], atmp[:, :], gel[:, :], AL.mult),
                          reads=[atmp, gel], writes=[(gc.name, cb)])
                gkeys = [(gc.name, cb) for cb in range(NCB)]
                for mb in range(KB):
                    ps = pw[mb % 2]

                    def emw2(e, mb=mb, ps=ps):
                        for cb in range(NCB):
                            inst = e.matmul(ps[:, :], w23[:, cb, mb * 128:(mb + 1) * 128], gc3[:, cb, :],
                                            start=(cb == 0), stop=(cb == NCB - 1))
                        return inst
                    bc.op("t", emw2, reads=[w2] + gkeys, writes=[ps])
                    bc.op("v", lambda e, mb=mb, ps=ps: e.scalar_tensor_tensor(
                        st3[:, mb, 1:TT + 1], ps[:, :],
                        bc.g_b2.ap().rearrange("p (a c) -> p a c", a=6)[:, l, mb:mb + 1],
                        st3[:, mb, 1:TT + 1], AL.add, AL.add),
                        reads=[ps, bc.g_b2] + stkeys, writes=[(stage.name, "y", mb)])
                ykeys = [(stage.name, "y", mb) for mb in range(KB)]
                bc.op("s", lambda e, c0=c0: e.dma_start(
                    sview(dram[dst].ap(), c0, c0 + TT), st3[:, :, 1:TT + 1]),
                    reads=ykeys, dma=True)




def kernel(**inputs):
    x = np.asarray(inputs["L0_attn_x"])
    B, T, _ = x.shape
    n_cores = 8
    cfg = {"T": T, "NSEQ": B // n_cores, "nlayer": 6}
    maps = host_layout(inputs, n_cores, T, cfg["nlayer"])
    nc = build_program(cfg)
    res = run_bass_kernel_spmd(nc, maps, core_ids=list(range(n_cores)))
    outs = []
    bpc = B // n_cores
    for r in res.results:
        o = r["outT"].reshape(D, bpc, T).transpose(1, 2, 0)
        outs.append(o)
    return np.ascontiguousarray(np.concatenate(outs, axis=0), dtype=np.float32)
